# version 04
# baseline (speedup 1.0000x reference)
import numpy as np
import ml_dtypes

import concourse.bass as bass
import concourse.bacc as bacc
import concourse.mybir as mybir
from concourse.bass_utils import run_bass_kernel_spmd
from contextlib import ExitStack
from concourse import dve_ops as _dve_ops
K_NR = _dve_ops.RECIPROCAL_APPROX_NR

F32 = mybir.dt.float32
F16 = mybir.dt.float16
FP8 = mybir.dt.float8e4
I8 = mybir.dt.int8
U8 = mybir.dt.uint8
ALU = mybir.AluOpType
ACTF = mybir.ActivationFunctionType

P = 128
TOK = 2048
TB = 512
NTB = TOK // TB
D = 1024
DC = D // P
HFF = 4096
HC = HFF // P
SFULL = 4096
NTC = SFULL // P
QB = 512
NQB = TOK // QB
NHP = 8

ENGINES = ("sync", "vector", "scalar", "tensor", "gpsimd")


class TokN:
    __slots__ = ("kind", "name", "idx", "val")

    def __init__(self, kind, name, idx, val):
        self.kind, self.name, self.idx, self.val = kind, name, idx, val


class OpN:
    __slots__ = ("engine", "emit", "deps", "is_dma", "group", "tok",
                 "inc_needed", "tag")

    def __init__(self, engine, emit, deps, is_dma, group, tok, tag):
        self.engine, self.emit, self.deps = engine, emit, deps
        self.is_dma, self.group, self.tok, self.tag = is_dma, group, tok, tag
        self.inc_needed = False


class Prog:

    def __init__(self):
        self.ops = []
        self.count = {e: 0 for e in ENGINES}
        self.dma_groups = {}

    def op(self, engine, emit, deps=(), tag=""):
        deps = [d for d in deps if d is not None]
        for d in deps:
            assert d.idx < len(self.ops), f"dep not earlier: {tag}"
        self.count[engine] += 1
        tok = TokN("op", engine, len(self.ops), self.count[engine])
        self.ops.append(OpN(engine, emit, deps, False, None, tok, tag))
        return tok

    def dma(self, queue, out, in_, deps=(), group="g0", tag=""):
        deps = [d for d in deps if d is not None]
        for d in deps:
            assert d.idx < len(self.ops), f"dep not earlier: {tag}"
        self.dma_groups[group] = self.dma_groups.get(group, 0) + 1
        tok = TokN("dma", group, len(self.ops), 16 * self.dma_groups[group])

        def emit(eng, out=out, in_=in_):
            return eng.dma_start(out, in_)

        self.ops.append(OpN(queue, emit, deps, True, group, tok, tag))
        return tok

    def barrier_tokens(self, toks):
        for t in toks:
            if t is not None:
                self.ops[t.idx].inc_needed = True

    def sem_names(self):
        return (["op_" + e for e in ENGINES if self.count[e]]
                + ["dma_" + g for g in self.dma_groups])

    def build(self, block, sems):
        for o in self.ops:
            for d in o.deps:
                self.ops[d.idx].inc_needed = True
        grp_needed = {o.group for o in self.ops if o.is_dma and o.inc_needed}
        cnt = {}
        for o in self.ops:
            if o.is_dma:
                if o.group in grp_needed:
                    o.inc_needed = True
                    k = ("dma", o.group)
                    cnt[k] = cnt.get(k, 0) + 1
                    o.tok.val = 16 * cnt[k]
            elif o.inc_needed:
                k = ("op", o.engine)
                cnt[k] = cnt.get(k, 0) + 1
                o.tok.val = cnt[k]
        per_engine = {e: [] for e in ENGINES}
        for o in self.ops:
            per_engine[o.engine].append(o)
        watermark = {e: {} for e in ENGINES}
        emit_plan = {e: [] for e in ENGINES}
        for e in ENGINES:
            for o in per_engine[e]:
                merged = {}
                for d in o.deps:
                    sn = ("dma_" + d.name) if d.kind == "dma" else ("op_" + d.name)
                    if d.val > watermark[e].get(sn, -1):
                        merged[sn] = max(merged.get(sn, 0), d.val)
                        watermark[e][sn] = d.val
                emit_plan[e].append((sorted(merged.items()), o))

        def mk_body(e):
            def body(eng):
                for waits, o in emit_plan[e]:
                    for sn, val in waits:
                        eng.wait_ge(sems[sn], val)
                    try:
                        inst = o.emit(eng)
                    except Exception as ex:
                        raise RuntimeError(f"emit failed at {o.tag}: {ex}") from ex
                    if o.inc_needed:
                        assert inst is not None, f"inc-needed {o.tag} gave None"
                        sn = ("dma_" + o.group) if o.is_dma else ("op_" + o.engine)
                        inst.then_inc(sems[sn], 16 if o.is_dma else 1)

            return body

        for e in ENGINES:
            if per_engine[e]:
                getattr(block, e)(mk_body(e))


class Arena:
    def __init__(self, handle, nbytes):
        self.h, self.nbytes = handle, nbytes
        self._cast = {}

    def _bt(self, dtype):
        if dtype not in self._cast:
            self._cast[dtype] = self.h if dtype == U8 else self.h.bitcast(dtype)
        return self._cast[dtype]

    def view(self, off_bytes, dtype, rows, cols, row0=0):
        esz = mybir.dt.size(dtype)
        assert off_bytes % esz == 0, (off_bytes, dtype)
        assert off_bytes + cols * esz <= self.nbytes, (off_bytes, cols, dtype)
        assert row0 + rows <= P
        bt = self._bt(dtype)
        re = self.nbytes // esz
        return bass.AP(bt, row0 * re + off_bytes // esz,
                       [[re, rows], [1, cols]])

    def bview(self, off_bytes, dtype, rows, cols, row0=0):
        esz = mybir.dt.size(dtype)
        bt = self._bt(dtype)
        re = self.nbytes // esz
        return bass.AP(bt, row0 * re + off_bytes // esz,
                       [[re, 1], [0, rows], [1, cols]])

    def view3(self, off_bytes, dtype, rows, shape2, strides2, row0=0):
        esz = mybir.dt.size(dtype)
        bt = self._bt(dtype)
        re = self.nbytes // esz
        return bass.AP(bt, row0 * re + off_bytes // esz,
                       [[re, rows]] + [[s, n] for s, n in
                                       zip(strides2, shape2)])


def build_nc():
    nc = bacc.Bacc("TRN2", target_bir_lowering=False)

    dxin = nc.dram_tensor("x", [D, TOK], F32, kind="ExternalInput").ap()
    dw1t = nc.dram_tensor("w1t", [D, HFF], F32, kind="ExternalInput").ap()
    dw2t = nc.dram_tensor("w2t", [HFF, D], F32, kind="ExternalInput").ap()
    dwqkvT = nc.dram_tensor("wqkvT", [D, 3 * D], F32,
                            kind="ExternalInput").ap()
    dwpT = nc.dram_tensor("wpT", [D, D], F32, kind="ExternalInput").ap()
    dbqkv = nc.dram_tensor("bqkv", [P, 24], F32, kind="ExternalInput").ap()
    dbp = nc.dram_tensor("bp", [P, 8], F32, kind="ExternalInput").ap()
    db1 = nc.dram_tensor("b1", [P, 32], F32, kind="ExternalInput").ap()
    db2 = nc.dram_tensor("b2", [P, 8], F32, kind="ExternalInput").ap()
    dsw = nc.dram_tensor("sw", [P, 24], F32, kind="ExternalInput").ap()
    dswp = nc.dram_tensor("swp", [P, 8], F32, kind="ExternalInput").ap()
    dswrow = nc.dram_tensor("swrow", [1, 3 * D], F32,
                            kind="ExternalInput").ap()
    dzwrow = nc.dram_tensor("zwrow", [1, 3 * D], F32,
                            kind="ExternalInput").ap()
    dswprow = nc.dram_tensor("swprow", [1, D], F32,
                             kind="ExternalInput").ap()
    dzwprow = nc.dram_tensor("zwprow", [1, D], F32,
                             kind="ExternalInput").ap()
    dbqrow = nc.dram_tensor("bqrow", [1, 3 * D], F32, kind="ExternalInput").ap()
    dscal = nc.dram_tensor("scal", [1, 3], F32, kind="ExternalInput").ap()
    dy = nc.dram_tensor("y", [D, TOK], F32, kind="ExternalOutput").ap()

    xs1 = nc.dram_tensor("xs1", [DC, P, TOK], F32).ap()
    xs2 = nc.dram_tensor("xs2", [DC, P, TOK], F32).ap()
    kscr = nc.dram_tensor("kscr", [DC, P, SFULL], F16).ap()
    w1scr = nc.dram_tensor("w1scr", [DC, P, HFF], F16).ap()
    w2scr = nc.dram_tensor("w2scr", [HC, P, D], FP8).ap()

    with ExitStack() as es:
        ee = es.enter_context
        AR = 212832
        art = ee(nc.sbuf_tensor("arena", [P, AR], U8))
        ar = Arena(art, AR)

        O_SCAL = 0
        O_SROW = 864
        O_BC = 2912
        O_XIO = 7008
        O_XQ = 23392
        O_HQ = O_XQ + 8192
        O_XQF = O_XQ + 16384
        O_XR = O_XQ + 24576
        O_Q = O_XR + 66560
        O_WQT = O_Q + 32768
        O_WPT = O_WQT + 49152
        AR_END = O_WPT + 16384
        assert AR_END <= AR, AR_END

        ones32 = ar.view(0, F32, P, 1)
        v_sa = ar.view(4, F32, P, 1)
        v_sattn = ar.view(8, F32, P, 1)
        v_c8 = ar.view(12, F32, P, 1)
        v_r50 = ar.view(16, F32, P, 1)
        v_sw = ar.view(20, F32, P, 24)
        v_scq = ar.view(308, F32, P, 24)
        v_bq = ar.view(404, F32, P, 24)
        v_swp = ar.view(500, F32, P, 8)
        v_scp = ar.view(596, F32, P, 8)
        v_bp = ar.view(628, F32, P, 8)
        v_b1 = ar.view(660, F32, P, 32)
        v_b2q = ar.view(788, F32, P, 8)
        rdys = ar.view(820, F16, P, 4)
        rdyd = ar.view(828, F16, P, 4)
        scal_t = ar.view(836, F32, 1, 3)
        scal2 = ar.view(848, F32, 1, 4)
        ones16 = rdys[0:P, 0:1]
        eps_t = scal2[0:1, 3:4]

        def srow0(cols=TB, row0=0):
            return ar.view(O_SROW, F32, 1, cols, row0=row0)

        vrow = ar.view(O_SROW, F32, 1, TB, row0=0)

        rs_b = ar.view(O_BC, F32, P, TB)
        mrs_b = ar.view(O_BC + 2048, F32, P, TB)
        zbA = ar.view(O_BC, F32, 64, QB)
        zbB = ar.view(O_BC + 2048, F32, 64, QB)
        kstg = [ar.view(O_BC, F16, P, TB), ar.view(O_BC + 1024, F16, P, TB)]
        vrow2 = ar.view(O_BC + 2048, F32, 1, TB)

        def xio(j):
            return ar.view(O_XIO + j * 2048, F32, P, TB)

        ksend = ar.view(O_XIO, F16, P, TOK)
        wstg = [ar.view(O_XIO, F32, P, 2048),
                ar.view(O_XIO + 8192, F32, P, 2048)]
        wstg4 = [ar.view(O_XIO, F32, P, HFF),
                 ar.view(O_XQ, F32, P, HFF)]

        def xlnv(j):
            return ar.view(O_XQ + j * 1024, F16, P, TB)

        def xqi8(j):
            return ar.view(O_XQ + j * 2048, I8, P, TOK)

        i8st = ar.view(O_XQ, I8, P, 2048)
        est = ar.view(O_XQ + 4096, F16, P, 2048)
        vev2 = [ar.view(O_WPT + 7168, F32, P, TB),
                ar.view(O_WPT + 9216, F32, P, TB)]
        vev3 = [ar.view(O_XIO + 8192, F32, P, TB),
                ar.view(O_XIO + 10240, F32, P, TB)]

        def hq_v(c):
            return ar.view(O_HQ + c * TB, FP8, P, TB)

        def oq16(jf):
            return ar.view(O_HQ + 1024 * jf, F16, P, QB)

        vsc_b = [ar.view(O_XIO, F32, P, TB),
                 ar.view(O_XIO + 2048, F32, P, TB)]
        vbc_b = [ar.view(O_XIO + 4096, F32, P, TB),
                 ar.view(O_XIO + 6144, F32, P, TB)]

        def xqf(j):
            return ar.view(O_XQF + j * 1024, F16, P, TB)


        def w1t(j):
            return ar.view(O_XR + j * 8192, F16, P, HFF)

        def vtile(tc):
            return ar.view(O_XR + tc * 2080, F16, P, 1040)

        def vslice(tc, h):
            return ar.view(O_XR + tc * 2080 + 65 * h * 2, F16, P, 65)

        def vones(tc):
            return ar.view3(O_XR + tc * 2080 + 128, F16, P, [16], [65])

        def vblock(tc, vb):
            return ar.view3(O_XR + tc * 2080 + 65 * 8 * vb * 2,
                            F16, P, [8, 64], [65, 1])

        def qhp(hp):
            return ar.view(O_Q + hp * 4096, F16, P, TOK)

        w2buf = [ar.view(O_WPT + 5120, FP8, P, D),
                 ar.view(O_WPT + 6144, FP8, P, D)]

        def wqT(j):
            return ar.view(O_WQT + j * 6144, F16, P, 3 * D)

        def krecv(j):
            return ar.view(O_WQT + j * 4096, F16, P, TOK)

        def khp(par):
            return ar.view(O_WQT + par * 8192, F16, P, SFULL)

        def outiq(j):
            return ar.view(O_WQT + 32 * 1024 + j * 2048, I8, P, TOK)

        def wpT(j):
            return ar.view(O_WPT + j * 2048, F16, P, D)

        sqtmp = ar.view(O_WPT, F16, P, TB)
        t16b = [ar.view(O_WPT + 1024, F16, P, TB),
                ar.view(O_WPT + 2048, F16, P, TB)]

        pa = ee(nc.psum_tensor("pa", [P, 4096], F32))

        def pbk(bank, cols=512, rows=P, row0=0, col0=0):
            return bass.AP(pa, row0 * 4096 + bank * 512 + col0,
                           [[4096, rows], [1, cols]])

        rsem_r = ee(nc.semaphore("rsem_r"))
        rsem_k = ee(nc.semaphore("rsem_k"))
        rsem_v = ee(nc.semaphore("rsem_v"))
        lsem = ee(nc.semaphore("lsem"))
        lsemv = ee(nc.semaphore("lsemv"))

        block = ee(nc.Block())
        p = Prog()
        RD = [(0, 1)] + [None] * 7

        g_iv = "iv"
        for dst, src in [(scal_t, dscal), (v_sw, dsw),
                         (v_swp, dswp), (v_bq, dbqkv),
                         (v_bp, dbp), (v_b1, db1), (v_b2q, db2)]:
            iv_last = p.dma("sync", dst, src, group=g_iv, tag="iv")

        c_o32 = p.op("gpsimd", lambda g: g.memset(ones32, 1.0), tag="o32")
        c_eps = p.op("gpsimd", lambda g: g.memset(eps_t, 1e-5), tag="eps")
        c_rdy = p.op("gpsimd", lambda g: g.memset(rdys, 1.0), tag="rdys")
        c_o16 = c_rdy

        s_d1 = p.op("vector", lambda v: v.tensor_scalar(
            scal2[0:1, 0:1], scal_t[0:1, 1:2], 8.0, None, ALU.mult),
            deps=[iv_last], tag="d1")
        s_d2 = p.op("vector", lambda v: v.reciprocal(
            scal2[0:1, 0:1], scal2[0:1, 0:1]), deps=[s_d1], tag="d2")
        s_d3 = p.op("vector", lambda v: v.reciprocal(
            scal2[0:1, 1:2], scal_t[0:1, 2:3]), deps=[iv_last], tag="d3")
        s_d4 = p.op("vector", lambda v: v.reciprocal(
            scal2[0:1, 2:3], scal_t[0:1, 0:1]), deps=[iv_last], tag="d4")
        b_sa = p.dma("sync", v_sa, ar.bview(836, F32, P, 1),
                     deps=[iv_last], group="cb1", tag="b_sa")
        b_st = p.dma("sync", v_sattn, ar.bview(840, F32, P, 1),
                     deps=[iv_last], group="cb2", tag="b_st")
        b_c8 = p.dma("sync", v_c8, ar.bview(848, F32, P, 1),
                     deps=[s_d2], group="cb3", tag="b_c8")
        b_r50 = p.dma("sync", v_r50, ar.bview(852, F32, P, 1),
                      deps=[s_d3], group="cb4", tag="b_r50")
        b_sap = p.dma("sync", v_scp[:, 0:1], ar.bview(844, F32, P, 1),
                      deps=[iv_last], group="cb5", tag="b_sap")

        s_scq = p.op("vector", lambda v: v.tensor_scalar(
            v_scq, v_sw, v_sa, None, ALU.mult), deps=[iv_last, b_sa],
            tag="scq")
        s_scq2 = p.op("vector", lambda v: v.tensor_scalar(
            v_scq[:, 0:8], v_scq[:, 0:8], v_c8, None, ALU.mult),
            deps=[s_scq, b_c8], tag="scq2")
        s_bq2 = p.op("vector", lambda v: v.tensor_scalar(
            v_bq[:, 0:8], v_bq[:, 0:8], v_c8, None, ALU.mult),
            deps=[iv_last, b_c8], tag="bq2")
        s_scp_a = p.op("vector", lambda v: v.tensor_scalar(
            v_scp[:, 1:8], v_swp[:, 1:8], v_scp[:, 0:1], None, ALU.mult),
            deps=[iv_last, b_sap], tag="scp_a")
        s_scp = p.op("vector", lambda v: v.tensor_tensor(
            v_scp[:, 0:1], v_swp[:, 0:1], v_scp[:, 0:1], ALU.mult),
            deps=[s_scp_a], tag="scp")
        s_b2a = p.op("vector", lambda v: v.tensor_copy(sqtmp[:, 0:8], v_b2q),
                     deps=[iv_last], tag="b2a")
        s_b2b = p.op("vector", lambda v: v.tensor_copy(v_b2q, sqtmp[:, 0:8]),
                     deps=[s_b2a], tag="b2b")

        last_c = [None, None]
        w1s_last = None
        for j in range(DC):
            par = j % 2
            ld = p.dma("sync", wstg4[par], dw1t[j * P:(j + 1) * P, :],
                       deps=[last_c[par]], group=f"wl{par}", tag=f"w1l{j}")
            cv = p.op("vector", lambda v, j=j, par=par: v.tensor_copy(
                w1t(j), wstg4[par]), deps=[ld], tag=f"w1c{j}")
            last_c[par] = cv
            w1s_last = p.dma("gpsimd", w1scr[j], w1t(j), deps=[cv],
                             group="w1s", tag=f"w1s{j}")
        w1_ready = [last_c[0], last_c[1]]

        last_c2 = [None, None]
        for c in range(HC):
            par = c % 2
            ld = p.dma("sync", wstg[par][:, 0:D], dw2t[c * P:(c + 1) * P, :],
                       deps=[last_c2[par], w1_ready[par]],
                       group=f"w2l{par}", tag=f"w2l{c}")
            cv = p.op("vector", lambda v, c=c, par=par: v.tensor_copy(
                w2buf[par], wstg[par][:, 0:D]), deps=[ld], tag=f"w2c{c}")
            stw = p.dma("gpsimd", w2scr[c], w2buf[par], deps=[cv],
                        group=f"w2s{par}", tag=f"w2s{c}")
            last_c2[par] = stw
        w2_ready = [last_c2[0], last_c2[1]]
        w2s_last = None

        def emit_ffn(xsrc, xdst, bar, phase, extra_load_dep, w2_prepped):
            store_toks = []
            prev_store = None
            info = {}
            pv = {}
            for tb in range(NTB):
                sl = slice(tb * TB, (tb + 1) * TB)
                bar0 = bar if tb == 0 else []
                lds = []
                for j in range(DC):
                    lds.append(p.dma(
                        "sync", xio(j), xsrc[j][:, sl],
                        deps=([prev_store] if prev_store else [])
                        + bar0 + ([extra_load_dep] if tb == 0 else []),
                        group=f"fx{phase}", tag=f"{phase}xl{tb}_{j}"))
                ldall = lds[-1]
                mm_sx = None
                for j in range(DC):
                    mm_sx = p.op("tensor", lambda t, j=j: t.matmul(
                        pbk(4, TB, 1, 0), ones32, xio(j),
                        start=(j == 0), stop=(j == DC - 1)),
                        deps=[ldall, c_o32, mm_sx] + bar0
                        + ([pv.get("mean"), pv.get("res")] if j == 0 else []),
                        tag=f"{phase}sx{tb}_{j}")
                mm_sq = pv.get("sq")
                for j in range(DC):
                    sq = p.op("vector", lambda v, j=j: v.tensor_mul(
                        sqtmp, xio(j), xio(j)),
                        deps=[ldall, mm_sq] + bar0, tag=f"{phase}sq{tb}_{j}")
                    mm_sq = p.op("tensor", lambda t, j=j: t.matmul(
                        pbk(5, TB, 1, 0), ones16, sqtmp,
                        start=(j == 0), stop=(j == DC - 1)),
                        deps=[sq, c_o16, mm_sx]
                        + ([pv.get("var")] if j == 0 else []),
                        tag=f"{phase}sq2{tb}_{j}")
                st_mean = p.op("vector", lambda v: v.tensor_scalar(
                    srow0(), pbk(4, TB, 1, 0), 1.0 / D, None, ALU.mult),
                    deps=[mm_sx, pv.get("mrs")],
                    tag=f"{phase}mean{tb}")
                st_m2 = p.op("vector", lambda v: v.tensor_mul(
                    rs_b[0:1, :], srow0(), srow0()),
                    deps=[st_mean, pv.get("bc_rs"), pv.get("ap")],
                    tag=f"{phase}m2{tb}")
                st_var = p.op("vector", lambda v: v.scalar_tensor_tensor(
                    rs_b[0:1, :], pbk(5, TB, 1, 0), 1.0 / D, rs_b[0:1, :],
                    ALU.mult, ALU.subtract), deps=[mm_sq, st_m2],
                    tag=f"{phase}var{tb}")
                st_ln = p.op("scalar", lambda s: s.activation(
                    mrs_b[0:1, :], rs_b[0:1, :], ACTF.Ln, bias=eps_t,
                    scale=1.0),
                    deps=[st_var, c_eps, pv.get("bc_mrs"), pv.get("ap")]
                    + bar0,
                    tag=f"{phase}lnv{tb}")
                st_rs = p.op("scalar", lambda s: s.activation(
                    mrs_b[0:1, :], mrs_b[0:1, :], ACTF.Exp, bias=0.0,
                    scale=-0.5),
                    deps=[st_ln], tag=f"{phase}rs{tb}")
                st_mrs = p.op("vector", lambda v: v.scalar_tensor_tensor(
                    srow0(), srow0(), -1.0, mrs_b[0:1, :],
                    ALU.mult, ALU.mult),
                    deps=[st_rs, st_mean], tag=f"{phase}mrs{tb}")
                bc_rs = p.dma("sync", rs_b,
                              ar.bview(O_BC + 2048, F32, P, TB),
                              deps=[st_rs, st_var, pv.get("ap")] + bar0,
                              group=f"brs{phase}", tag=f"{phase}brs{tb}")
                bc_mrs = p.dma("gpsimd", mrs_b,
                               ar.bview(O_SROW, F32, P, TB),
                               deps=[st_mrs, bc_rs],
                               group=f"bmrs{phase}", tag=f"{phase}bmrs{tb}")
                ap_toks = []
                for j in range(DC):
                    par = j % 2
                    a1 = p.op("vector", lambda v, j=j, par=par: v.tensor_mul(
                        t16b[par], xio(j), rs_b),
                        deps=[ldall, bc_rs]
                        + ([ap_toks[j - 2]] if j >= 2 else []),
                        tag=f"{phase}ap1{tb}_{j}")
                    a2 = p.op("vector", lambda v, j=j, par=par:
                              v.tensor_tensor(xlnv(j), t16b[par], mrs_b,
                                              ALU.add),
                              deps=[a1, bc_mrs] + bar0,
                              tag=f"{phase}ap2{tb}_{j}")
                    ap_toks.append(a2)
                hq_toks = []
                last_fa = None
                for c in range(HC):
                    par = c % 2
                    mm = None
                    for j in range(DC):
                        mm = p.op(
                            "tensor", lambda t, c=c, j=j, par=par: t.matmul(
                                pbk(par, TB), w1t(j)[:, c * P:(c + 1) * P],
                                xlnv(j), start=(j == 0), stop=(j == DC - 1)),
                            deps=[ap_toks[j], mm]
                            + ([hq_toks[c - 2]] if c >= 2 else [])
                            + (bar0 + [pv.get("res")]
                               if c == 0 and j == 0 else []),
                            tag=f"{phase}fa{tb}_{c}_{j}")
                    last_fa = mm
                    hq_toks.append(p.op(
                        "scalar", lambda s, c=c, par=par: s.activation(
                            hq_v(c), pbk(par, TB), ACTF.Relu,
                            bias=v_b1[:, c:c + 1], scale=1.0),
                        deps=[mm] + (bar0 if c == 0 else []),
                        tag=f"{phase}hq{tb}_{c}"))
                res_toks = []
                w2l_last = [pv.get("w2a"), pv.get("w2b")]
                mms = [None] * DC
                for c in range(HC):
                    par = c % 2
                    w2l = p.dma("sync", w2buf[par], w2scr[c],
                                deps=[w2l_last[par]] + list(w2_prepped)
                                + ([hq_toks[1]] if c < 2 else []),
                                group=f"w2f{phase}{par}", tag=f"{phase}w2l{tb}_{c}")
                    for jd in range(DC):
                        mms[jd] = p.op(
                            "tensor", lambda t, jd=jd, c=c, par=par: t.matmul(
                                pbk(jd, TB),
                                w2buf[par][:, jd * P:(jd + 1) * P], hq_v(c),
                                start=(c == 0), stop=(c == HC - 1)),
                            deps=[w2l, hq_toks[c], mms[jd]]
                            + ([hq_toks[-1], st_mean, st_var,
                                pv.get("res")] if c == 0 and jd == 0 else []),
                            tag=f"{phase}fb{tb}_{c}_{jd}")
                    w2l_last[par] = mms[DC - 1]
                for jd in range(DC):
                    par = jd % 2
                    t1 = p.op("vector", lambda v, jd=jd, par=par:
                              v.tensor_scalar(t16b[par],
                                              pbk(jd, TB),
                                              v_b2q[:, jd:jd + 1], None,
                                              ALU.add),
                              deps=[mms[jd], ap_toks[-1]]
                              + ([res_toks[jd - 2]] if jd >= 2 else []),
                              tag=f"{phase}q16{tb}_{jd}")
                    t2 = p.op("vector", lambda v, jd=jd, par=par:
                              v.scalar_tensor_tensor(
                                  xio(jd), t16b[par], 0.5, xio(jd),
                                  ALU.mult, ALU.add),
                              deps=[t1], tag=f"{phase}res{tb}_{jd}")
                    res_toks.append(t2)
                    store_toks.append(p.dma(
                        "gpsimd", xdst[jd][:, sl], xio(jd), deps=[t2],
                        group=f"fs{phase}", tag=f"{phase}xst{tb}_{jd}"))
                prev_store = store_toks[-1]
                pv = dict(mean=st_mean, m2=st_m2, var=st_var, mrs=st_mrs,
                          bc_rs=bc_rs, bc_mrs=bc_mrs, ap=ap_toks[-1],
                          sq=mm_sq, res=res_toks[-1],
                          w2a=w2l_last[0], w2b=w2l_last[1])
                info = dict(last_fa=last_fa, last_res=res_toks[-1],
                            last_hq=hq_toks[-1], last_rs=st_rs,
                            last_bc=bc_mrs, last_sq=mm_sq)
            info["stores"] = store_toks
            return info

        xs1v = [xs1[j] for j in range(DC)]
        xs2v = [xs2[j] for j in range(DC)]
        dxv = [dxin[j * P:(j + 1) * P, :] for j in range(DC)]
        dyv = [dy[j * P:(j + 1) * P, :] for j in range(DC)]

        f1bar = [w1_ready[0], w1_ready[1], w2_ready[0], w2_ready[1],
                 iv_last, s_b2b, c_o32, c_o16, c_eps]
        f1 = emit_ffn(dxv, xs1v, f1bar, "F1", None, w2_ready)

        ph2bar = [f1["stores"][-1], f1["last_fa"], f1["last_res"],
                  f1["last_rs"], f1["last_bc"], f1["last_hq"]]
        xq_last = None
        prev = None
        last_a1 = [None, None]
        pv2 = {"mean": f1["last_bc"], "mrs": f1["last_bc"],
               "ap": f1["last_res"], "bc_rs": f1["last_bc"],
               "bc_mrs": f1["last_bc"], "sq": f1["last_sq"]}
        for tb in range(NTB):
            sl = slice(tb * TB, (tb + 1) * TB)
            bar0 = ph2bar if tb == 0 else []
            lds = []
            for j in range(DC):
                lds.append(p.dma("sync", xio(j), xs1[j][:, sl],
                                 deps=([prev] if prev else [])
                                 + ([f1["stores"][-1]] if tb == 0 else []),
                                 group="p2x", tag=f"p2xl{tb}_{j}"))
            ldall = lds[-1]
            mm_sx = None
            for j in range(DC):
                mm_sx = p.op("tensor", lambda t, j=j: t.matmul(
                    pbk(4, TB, 1, 0), ones32, xio(j),
                    start=(j == 0), stop=(j == DC - 1)),
                    deps=[ldall, mm_sx] + bar0
                    + ([pv2.get("mean")] if j == 0 else []),
                    tag=f"p2sx{tb}_{j}")
            mm_sq = pv2.get("sq")
            for j in range(DC):
                sq = p.op("vector", lambda v, j=j: v.tensor_mul(
                    sqtmp, xio(j), xio(j)),
                    deps=[ldall, mm_sq] + bar0, tag=f"p2sq{tb}_{j}")
                mm_sq = p.op("tensor", lambda t, j=j: t.matmul(
                    pbk(5, TB, 1, 0), ones16, sqtmp,
                    start=(j == 0), stop=(j == DC - 1)),
                    deps=[sq, mm_sx]
                    + ([pv2.get("var")] if j == 0 else []),
                    tag=f"p2sq2{tb}_{j}")
            st_mean = p.op("vector", lambda v: v.tensor_scalar(
                srow0(), pbk(4, TB, 1, 0), 1.0 / D, None, ALU.mult),
                deps=[mm_sx, pv2.get("mrs")],
                tag=f"p2mean{tb}")
            st_m2 = p.op("vector", lambda v: v.tensor_mul(
                rs_b[0:1, :], srow0(), srow0()),
                deps=[st_mean, pv2.get("bc_rs"), pv2.get("ap")],
                tag=f"p2m2{tb}")
            st_var = p.op("vector", lambda v: v.scalar_tensor_tensor(
                rs_b[0:1, :], pbk(5, TB, 1, 0), 1.0 / D, rs_b[0:1, :],
                ALU.mult, ALU.subtract), deps=[mm_sq, st_m2],
                tag=f"p2var{tb}")
            st_ln = p.op("scalar", lambda s: s.activation(
                mrs_b[0:1, :], rs_b[0:1, :], ACTF.Ln, bias=eps_t, scale=1.0),
                deps=[st_var, pv2.get("bc_mrs"), pv2.get("ap")] + bar0,
                tag=f"p2lnv{tb}")
            st_rs = p.op("scalar", lambda s: s.activation(
                mrs_b[0:1, :], mrs_b[0:1, :], ACTF.Exp, bias=0.0,
                scale=-0.5),
                deps=[st_ln], tag=f"p2rs{tb}")
            st_rs2 = p.op("vector", lambda v: v.tensor_scalar(
                mrs_b[0:1, :], mrs_b[0:1, :], scal2[0:1, 2:3], None,
                ALU.mult),
                deps=[st_rs, s_d4], tag=f"p2rs2{tb}")
            st_mrs = p.op("vector", lambda v: v.scalar_tensor_tensor(
                srow0(), srow0(), -1.0, mrs_b[0:1, :], ALU.mult, ALU.mult),
                deps=[st_rs2, st_mean], tag=f"p2mrs{tb}")
            bc_rs = p.dma("sync", rs_b,
                          ar.bview(O_BC + 2048, F32, P, TB),
                          deps=[st_rs2, st_var, pv2.get("ap")] + bar0,
                          group="p2brs", tag=f"p2brs{tb}")
            bc_mrs = p.dma("gpsimd", mrs_b,
                           ar.bview(O_SROW, F32, P, TB),
                           deps=[st_mrs, bc_rs],
                           group="p2bmrs", tag=f"p2bmrs{tb}")
            for j in range(DC):
                par = j % 2
                a1 = p.op("vector", lambda v, j=j, par=par: v.tensor_mul(
                    vev2[par], xio(j), rs_b),
                    deps=[ldall, bc_rs, last_a1[par]], tag=f"p2a1{tb}_{j}")
                a2 = p.op("vector", lambda v, j=j, par=par, sl=sl:
                          v.tensor_tensor(xqi8(j)[:, sl], vev2[par], mrs_b,
                                          ALU.add),
                          deps=[a1, bc_mrs] + bar0, tag=f"p2a2{tb}_{j}")
                last_a1[par] = a2
                xq_last = a2
            prev = xq_last
            pv2 = dict(mean=st_mean, m2=st_m2, var=st_var, mrs=st_mrs,
                       bc_rs=bc_rs, bc_mrs=bc_mrs, ap=xq_last, sq=mm_sq)

        zwB = ar.view(O_XR, F32, P, 3 * D)
        zwpB = ar.view(O_XR + 12288, F32, P, D)
        rswB = ar.view(O_XR + 16384, F32, P, 3 * D)
        rswpB = ar.view(O_XR + 28672, F32, P, D)
        xr_free = [f1["last_fa"], w1s_last]
        rowlds = []
        for gi, (tile, off, cols, dsrc, dep) in enumerate([
                (rswB, O_XR + 16384, 3 * D, dswrow, xr_free),
                (rswpB, O_XR + 28672, D, dswprow, xr_free),
                (zwB, O_XR, 3 * D, dzwrow, xr_free),
                (zwpB, O_XR + 12288, D, dzwprow, xr_free)]):
            rl = p.dma("sync", ar.view(off, F32, 1, cols), dsrc,
                       deps=dep + [xq_last], group=f"rw{gi}",
                       tag=f"rwl{gi}")
            if gi < 2:
                rl = p.op("vector", lambda v, off=off, cols=cols:
                          v.reciprocal(ar.view(off, F32, 1, cols),
                                       ar.view(off, F32, 1, cols)),
                          deps=[rl], tag=f"rwr{gi}")
            rowlds.append(p.dma(
                "gpsimd", ar.view(off, F32, P - 1, cols, row0=1),
                ar.bview(off, F32, P - 1, cols),
                deps=[rl], group=f"rb{gi}", tag=f"rwb{gi}"))
        rows_ready = rowlds

        wstg_q = [ar.view(O_XIO, F32, P, D),
                  ar.view(O_XIO + 8192, F32, P, D)]
        u8t = [ar.view(O_XIO + 4096, U8, P, D),
               ar.view(O_XIO + 12288, U8, P, D)]
        wtmp = [ar.view(O_XQF, F32, P, D),
                ar.view(O_XQF + 4096, F32, P, D)]
        pieces = [(False, j, pc) for j in range(DC) for pc in range(3)]
        pieces += [(True, j, 0) for j in range(DC)]
        last_ld = [None, None]
        last_t1 = [None, None]
        last_t2 = [None, None]
        last_t3 = [None, None]
        for idx, (is_p, j, pc) in enumerate(pieces):
            par = idx % 2
            if is_p:
                srcd = dwpT[j * P:(j + 1) * P, :]
                rsB, zB = rswpB, zwpB
                dst = wpT(j)
            else:
                srcd = dwqkvT[j * P:(j + 1) * P, pc * D:(pc + 1) * D]
                rsB = rswB[0:P, pc * D:(pc + 1) * D]
                zB = zwB[0:P, pc * D:(pc + 1) * D]
                dst = wqT(j)[:, pc * D:(pc + 1) * D]
            ld = p.dma("sync", wstg_q[par], srcd,
                       deps=[last_t1[par]]
                       + ([xq_last] if idx < 2 else []),
                       group=f"wq{par}", tag=f"wql{idx}")
            t1 = p.op("vector", lambda v, par=par, rsB=rsB:
                      v.tensor_tensor(wtmp[par], wstg_q[par], rsB,
                                      ALU.mult),
                      deps=[ld, last_t2[par]] + rows_ready
                      + ([xq_last] if idx < 2 else []),
                      tag=f"wq1{idx}")
            t2 = p.op("vector", lambda v, par=par, zB=zB:
                      v.tensor_tensor(u8t[par], wtmp[par], zB, ALU.add),
                      deps=[t1, last_t3[par]], tag=f"wq2{idx}")
            t3 = p.op("vector", lambda v, par=par, zB=zB, dst=dst:
                      v.tensor_tensor(dst, u8t[par], zB, ALU.subtract),
                      deps=[t2], tag=f"wq3{idx}")
            last_ld[par] = ld
            last_t1[par] = t1
            last_t2[par] = t2
            last_t3[par] = t3
        wq_ready = p.op("gpsimd", lambda g: g.engine_nop(),
                        deps=[last_t3[0], last_t3[1]], tag="wqrdy")

        vsc_toks = []
        prev_vd = None
        for vb in range(2):
            d1 = p.dma("sync", vrow,
                       dswrow[0:1, 2048 + vb * TB:2048 + (vb + 1) * TB],
                       deps=[prev_vd], group="vscd", tag=f"vswl{vb}")
            r1 = p.op("vector", lambda v: v.tensor_scalar(
                vrow2, vrow, scal_t[0:1, 0:1], None, ALU.mult),
                deps=[d1, iv_last, xq_last]
                + ([vsc_toks[0]] if vb == 1 else []), tag=f"vsc{vb}")
            b1t = p.dma("sync", vsc_b[vb],
                        ar.bview(O_BC + 2048, F32, P, TB),
                        deps=[r1, xq_last, wq_ready], group="vscb",
                        tag=f"vscb{vb}")
            d2 = p.dma("sync", vrow,
                       dbqrow[0:1, 2048 + vb * TB:2048 + (vb + 1) * TB],
                       deps=[r1], group="vscd", tag=f"vbql{vb}")
            b2t = p.dma("gpsimd", vbc_b[vb],
                        ar.bview(O_SROW, F32, P, TB),
                        deps=[d2, xq_last, wq_ready], group="vbcb",
                        tag=f"vbcb{vb}")
            prev_vd = b2t
            vsc_toks += [b1t, b2t, r1]

        kscr_toks = []
        q_toks = []
        vev_toks = []
        vone_toks = []
        last_kst = [None, None]
        last_mmp = [None, None]
        last_vps = [None, None]
        last_xqf = None
        last_vevb = [None, None]
        for tb in range(NTB):
            sl = slice(tb * TB, (tb + 1) * TB)
            cvs = []
            for j in range(DC):
                cvs.append(p.op(
                    "vector", lambda v, j=j, sl=sl: v.tensor_copy(
                        xqf(j), xqi8(j)[:, sl]),
                    deps=[xq_last, wq_ready]
                    + ([last_xqf] if last_xqf else []),
                    tag=f"p3cv{tb}_{j}"))
            cvall = cvs[-1]
            for which, base in (("k", 8), ("q", 0)):
                for jf in range(DC):
                    par = jf % 2
                    mm = None
                    for j in range(DC):
                        mm = p.op(
                            "tensor",
                            lambda t, jf=jf, j=j, par=par, base=base:
                            t.matmul(pbk(par, TB),
                                     wqT(j)[:, (base + jf) * P:
                                            (base + jf + 1) * P],
                                     xqf(j), start=(j == 0),
                                     stop=(j == DC - 1)),
                            deps=[cvall, mm, last_mmp[par], wq_ready],
                            tag=f"p3{which}mm{tb}_{jf}_{j}")
                    if which == "k":
                        ev = p.op(
                            "vector", lambda v, jf=jf, par=par:
                            v.tensor_scalar(kstg[par], pbk(par, TB),
                                            v_scq[:, 8 + jf:9 + jf],
                                            v_bq[:, 8 + jf:9 + jf],
                                            ALU.mult, ALU.add),
                            deps=[mm, last_kst[par], s_scq2, s_bq2, xq_last],
                            tag=f"p3kev{tb}_{jf}")
                        stw = p.dma("gpsimd", kscr[jf][:, sl],
                                    kstg[par], deps=[ev], group=f"ks{par}",
                                    tag=f"p3kst{tb}_{jf}")
                        kscr_toks.append(stw)
                        last_kst[par] = stw
                        last_mmp[par] = ev
                    else:
                        ev = p.op(
                            "vector", lambda v, jf=jf, par=par, sl=sl:
                            v.tensor_scalar(qhp(jf)[:, sl], pbk(par, TB),
                                            v_scq[:, jf:jf + 1],
                                            v_bq[:, jf:jf + 1],
                                            ALU.mult, ALU.add),
                            deps=[mm, s_scq2, s_bq2],
                            tag=f"p3qev{tb}_{jf}")
                        q_toks.append(ev)
                        last_mmp[par] = ev
            for ti in range(TB // P):
                tc = tb * (TB // P) + ti
                von = p.op("gpsimd", lambda g, tc=tc: g.memset(
                    vones(tc), 1.0),
                    deps=[f1["last_fa"], w1s_last, xq_last, wq_ready],
                    tag=f"p3von{tc}")
                vone_toks.append(von)
                for vb in range(2):
                    par = (ti * 2 + vb) % 2
                    mm = None
                    for j in range(DC):
                        mm = p.op(
                            "tensor",
                            lambda t, ti=ti, vb=vb, j=j, par=par: t.matmul(
                                pbk(2 + par, TB),
                                xqf(j)[:, ti * P:(ti + 1) * P],
                                wqT(j)[:, 2048 + vb * TB:
                                       2048 + (vb + 1) * TB],
                                start=(j == 0), stop=(j == DC - 1)),
                            deps=[cvall, mm, last_vps[par], wq_ready],
                            tag=f"p3vmm{tc}_{vb}_{j}")
                    e1 = p.op("vector", lambda v, par=par, vb=vb:
                              v.tensor_tensor(vev3[par], pbk(2 + par, TB),
                                              vsc_b[vb], ALU.mult),
                              deps=[mm, vsc_toks[0], vsc_toks[3],
                                    last_vevb[par]],
                              tag=f"p3ve1{tc}_{vb}")
                    e2 = p.op(
                        "vector", lambda v, tc=tc, vb=vb, par=par:
                        v.tensor_tensor(
                            vblock(tc, vb), vev3[par], vbc_b[vb], ALU.add),
                        deps=[e1, f1["last_fa"], w1s_last,
                              vsc_toks[1], vsc_toks[4]],
                        tag=f"p3ve2{tc}_{vb}")
                    vev_toks.append(e2)
                    last_vps[par] = e1
                    last_vevb[par] = e2
            last_xqf = vev_toks[-1]
            last_cv_tok = cvall

        q_all = q_toks[-1]
        ks_par = {0: None, 1: None}
        for t_ in kscr_toks:
            ks_par[0 if t_.name == "ks0" else 1] = t_
        kscr_all = [ks_par[0], ks_par[1]]
        v_all_done = vev_toks + vone_toks

        mm_done = [last_mmp[0], last_mmp[1], last_vps[0], last_vps[1],
                   q_all, vev_toks[-1], vone_toks[-1]] + kscr_all
        rdy_snd = p.op("gpsimd", lambda g: g.remote_dma_broadcast(
            rdyd, rdys, rsem_r, lsemv, rdests=RD),
            deps=mm_done + [c_rdy], tag="rdy_snd")
        rdy_trg = p.op("gpsimd", lambda g: g.trigger_dma(count=1),
                       deps=[rdy_snd], tag="rdy_trg")
        rdy_wt = p.op("gpsimd", lambda g: (g.wait_ge(rsem_r, 2),
                                           g.engine_nop())[1],
                      deps=[rdy_trg], tag="rdy_wt")

        vtrgs = []
        for i in range(16):
            snd = p.op("gpsimd", lambda g, i=i: g.remote_dma_broadcast(
                vtile(16 + i), vtile(i), rsem_v, lsemv, rdests=RD),
                deps=[rdy_wt], tag=f"vsnd{i}")
            vtrgs.append(p.op("gpsimd", lambda g: g.trigger_dma(count=1),
                              deps=[snd], tag=f"vtrg{i}"))

        k_fw = []
        prev_lw = rdy_wt
        for j in range(DC):
            ld = p.dma("sync", ksend, kscr[j][:, 0:TOK],
                       deps=kscr_all + [prev_lw], group="ksl",
                       tag=f"ksl{j}")
            snd = p.op("gpsimd", lambda g, j=j: g.remote_dma_broadcast(
                krecv(j), ksend, rsem_k, lsem, rdests=RD),
                deps=[ld, rdy_wt], tag=f"ksnd{j}")
            trg = p.op("gpsimd", lambda g: g.trigger_dma(count=1),
                       deps=[snd], tag=f"ktrg{j}")
            prev_lw = p.op("gpsimd", lambda g, j=j: (
                g.wait_ge(lsem, 16 * (j + 1)), g.engine_nop())[1],
                deps=[trg], tag=f"klw{j}")
        k_sent = prev_lw

        rkw = p.op("gpsimd", lambda g: (g.wait_ge(rsem_k, 16),
                                        g.engine_nop())[1],
                   deps=[k_sent], tag="rkw")
        for j in range(DC):
            k_fw.append(p.dma("sync", kscr[j][:, TOK:SFULL], krecv(j),
                              deps=[rkw], group="kfw", tag=f"kfw{j}"))
        kfw_all = k_fw[-1]
        rvw = p.op("gpsimd", lambda g: (g.wait_ge(rsem_v, 32),
                                        g.engine_nop())[1],
                   deps=[vtrgs[-1]], tag="rvw")
        vsend_done = p.op("gpsimd", lambda g: (g.wait_ge(lsemv, 272),
                                               g.engine_nop())[1],
                          deps=[vtrgs[-1]], tag="vsend_done")

        oiq_toks = []
        last_khp_use = [None, None]
        last_oiq = [None, None]
        av_last = None
        sc_last = None
        ex_last = None
        prev_exp = None
        prev_zread = [None, None]
        first_attn = [vev_toks[-1], vone_toks[-1], kfw_all, rvw]
        for hp in range(NHP):
            par = hp % 2
            klo = p.dma("sync", khp(par)[:, 0:TOK], kscr[hp][:, 0:TOK],
                        deps=kscr_all + [k_sent, kfw_all, last_khp_use[par]],
                        group=f"kho{par}", tag=f"klo{hp}")
            klp = p.dma("sync", khp(par)[:, TOK:SFULL],
                        kscr[hp][:, TOK:SFULL],
                        deps=[kfw_all, last_khp_use[par]],
                        group=f"khq{par}", tag=f"klp{hp}")
            for qb in range(NQB):
                qsl = slice(qb * QB, (qb + 1) * QB)
                qb_first = sc_last
                avA = avB = None
                a0s = {}
                for g in range(NTC // 2):
                    tc0, tc1 = 2 * g, 2 * g + 1
                    for tc in (tc0, tc1):
                        sp = tc % 2
                        kdep = klo if tc < 16 else klp
                        mmA = p.op(
                            "tensor", lambda t, par=par, tc=tc, sp=sp,
                            hp=hp, qsl=qsl: t.matmul(
                                pbk(2 * sp),
                                khp(par)[0:64, tc * P:(tc + 1) * P],
                                qhp(hp)[0:64, qsl], start=True, stop=True),
                            deps=[kdep, q_all, a0s.get(tc - 2),
                                  qb_first if tc <= 1 else None]
                            + (first_attn if hp == 0 and qb == 0 and g == 0
                               else []),
                            tag=f"sA{hp}_{qb}_{tc}")
                        mmB = p.op(
                            "tensor", lambda t, par=par, tc=tc, sp=sp,
                            hp=hp, qsl=qsl: t.matmul(
                                pbk(2 * sp + 1),
                                khp(par)[64:P, tc * P:(tc + 1) * P],
                                qhp(hp)[64:P, qsl], start=True, stop=True,
                                tile_position=(64, 0)),
                            deps=[mmA], tag=f"sB{hp}_{qb}_{tc}")
                        a0 = p.op(
                            "vector", lambda v, sp=sp: v.tensor_copy(
                                i8st[:, sp * 1024:(sp + 1) * 1024],
                                pbk(2 * sp, 1024)),
                            deps=[mmB, prev_exp]
                            + ([last_cv_tok] if hp == 0 and qb == 0
                               and g == 0 else []),
                            tag=f"a0_{hp}_{qb}_{tc}")
                        a0s[tc] = a0
                        sc_last = a0
                    ex = p.op(
                        "scalar", lambda s: s.activation(
                            est, i8st, ACTF.Exp, bias=0.0,
                            scale=v_sattn[0:P, 0:1]),
                        deps=[a0s[tc0], a0s[tc1], b_st,
                              avB if g > 0 else av_last,
                              vev_toks[-1] if hp == 0 and qb == 0 and g == 0
                              else None],
                        tag=f"ex{hp}_{qb}_{g}")
                    prev_exp = ex
                    ex_last = ex
                    avdepv = [rvw] if tc1 >= 16 else []
                    avA = p.op(
                        "tensor", lambda t, hp=hp, tc0=tc0, g=g: t.matmul(
                            pbk(4, QB, 65), vslice(tc0, 2 * hp),
                            est[:, 0:QB], start=(g == 0), stop=False),
                        deps=[ex, avA, last_oiq[0] if g == 0 else None]
                        + avdepv, tag=f"avA{hp}_{qb}_{tc0}")
                    avB = p.op(
                        "tensor", lambda t, hp=hp, tc0=tc0, g=g: t.matmul(
                            pbk(5, QB, 65), vslice(tc0, 2 * hp + 1),
                            est[:, QB:2 * QB], start=(g == 0), stop=False),
                        deps=[avA, last_oiq[1] if g == 0 else None],
                        tag=f"avB{hp}_{qb}_{tc0}")
                    avA = p.op(
                        "tensor", lambda t, hp=hp, tc1=tc1, g=g: t.matmul(
                            pbk(4, QB, 65), vslice(tc1, 2 * hp),
                            est[:, 1024:1024 + QB], start=False,
                            stop=(g == NTC // 2 - 1)),
                        deps=[avB], tag=f"avA{hp}_{qb}_{tc1}")
                    avB = p.op(
                        "tensor", lambda t, hp=hp, tc1=tc1, g=g: t.matmul(
                            pbk(5, QB, 65), vslice(tc1, 2 * hp + 1),
                            est[:, 1024 + QB:2048], start=False,
                            stop=(g == NTC // 2 - 1)),
                        deps=[avA], tag=f"avB{hp}_{qb}_{tc1}")
                av_last = avB
                zslA = ar.view(O_XQF + 6144, F32, 1, QB, row0=64)
                zscA = ar.view(O_XQF + 4096, F32, 1, QB, row0=64)
                zslB = ar.view(O_XQF + 2048, F32, 1, QB, row0=64)
                zscB = ar.view(O_XQF, F32, 1, QB, row0=64)
                rcA = p.op("vector", lambda v, zslA=zslA:
                           v.reciprocal(zslA, pbk(4, QB, 1, 64)),
                           deps=[avA, prev_zread[0]], tag=f"zrcA{hp}_{qb}")
                rcB = p.op("vector", lambda v, zslB=zslB:
                           v.reciprocal(zslB, pbk(5, QB, 1, 64)),
                           deps=[avB, rcA, prev_zread[1]],
                           tag=f"zrcB{hp}_{qb}")
                rsA = p.op("vector", lambda v, zslA=zslA: v.tensor_scalar(
                    zslA, zslA, v_r50[64:65, 0:1], None, ALU.mult),
                    deps=[rcA, b_r50], tag=f"zsA{hp}_{qb}")
                rsB = p.op("vector", lambda v, zslB=zslB: v.tensor_scalar(
                    zslB, zslB, v_r50[64:65, 0:1], None, ALU.mult),
                    deps=[rcB, b_r50], tag=f"zsB{hp}_{qb}")
                bcA = p.dma("sync", zbA,
                            ar.bview(O_XQF + 6144, F32, 64, QB, row0=64),
                            deps=[rsA, last_oiq[0]], group="zba",
                            tag=f"zbA{hp}_{qb}")
                bcB = p.dma("gpsimd", zbB,
                            ar.bview(O_XQF + 2048, F32, 64, QB, row0=64),
                            deps=[rsB, last_oiq[1]], group="zbb",
                            tag=f"zbB{hp}_{qb}")
                oA = p.op("vector", lambda v, hp=hp, qsl=qsl:
                          v.tensor_tensor(
                              outiq(hp)[0:64, qsl], pbk(4, QB, 64), zbA,
                              ALU.mult),
                          deps=[bcA, avB, kfw_all], tag=f"oA{hp}_{qb}")
                oB = p.op("vector", lambda v, hp=hp, qsl=qsl:
                          v.tensor_tensor(
                              outiq(hp)[64:P, qsl], pbk(5, QB, 64), zbB,
                              ALU.mult),
                          deps=[bcB, oA], tag=f"oB{hp}_{qb}")
                oiq_toks.append((oA, oB))
                last_oiq = [oA, oB]
                prev_zread = [bcA, bcB]
            last_khp_use[par] = sc_last

        prj_stores = []
        last_wave = None
        last_cvt = None
        prev_pst = None
        for qb in range(NQB):
            qsl = slice(qb * QB, (qb + 1) * QB)
            lds = []
            for j in range(DC):
                lds.append(p.dma(
                    "sync", xio(j), xs1[j][:, qsl],
                    deps=[k_sent, oiq_toks[-1][1]]
                    + ([prev_pst] if prev_pst else []),
                    group="px", tag=f"pxl{qb}_{j}"))
            ldall = lds[-1]
            badd = []
            for j in range(DC):
                badd.append(p.op(
                    "vector", lambda v, j=j: v.tensor_scalar(
                        xio(j), xio(j), v_bp[:, j:j + 1], None, ALU.add),
                    deps=[ldall, iv_last], tag=f"pb{qb}_{j}"))
            cvt = []
            for jf in range(DC):
                cvt.append(p.op(
                    "vector", lambda v, jf=jf, qsl=qsl: v.tensor_copy(
                        oq16(jf), outiq(jf)[:, qsl]),
                    deps=[oiq_toks[jf * NQB + qb][1], last_wave,
                          vsc_toks[1]],
                    tag=f"pcv{qb}_{jf}"))
            for w in range(2):
                mm = None
                mm_by_fc = {}
                for jf in range(DC):
                    for fc in range(4 * w, 4 * w + 4):
                        mm = p.op(
                            "tensor", lambda t, jf=jf, fc=fc, w=w: t.matmul(
                                pbk(fc - 4 * w, QB),
                                wpT(jf)[:, fc * P:(fc + 1) * P], oq16(jf),
                                start=(jf == 0), stop=(jf == DC - 1)),
                            deps=[cvt[jf], mm, wq_ready, last_wave]
                            + ([av_last, sc_last] if (qb, w, jf, fc) ==
                               (0, 0, 0, 0) else []),
                            tag=f"pmm{qb}_{w}_{jf}_{fc}")
                        mm_by_fc[fc] = mm
                evs = []
                for fc in range(4 * w, 4 * w + 4):
                    evs.append(p.op(
                        "vector", lambda v, fc=fc, w=w: v.scalar_tensor_tensor(
                            xio(fc), pbk(fc - 4 * w, QB),
                            v_scp[:, fc:fc + 1], xio(fc),
                            ALU.mult, ALU.add),
                        deps=[mm, badd[fc], s_scp], tag=f"pev{qb}_{w}_{fc}"))
                last_wave = evs[-1]
                for fc in range(4 * w, 4 * w + 4):
                    prj_stores.append(p.dma(
                        "gpsimd", xs2[fc][:, qsl], xio(fc),
                        deps=[evs[fc - 4 * w]], group="ps",
                        tag=f"pst{qb}_{fc}"))
            prev_pst = prj_stores[-1]

        w1r_last = None
        for j in range(DC):
            w1r_last = p.dma("sync", w1t(j), w1scr[j],
                             deps=[av_last, w1s_last, vsend_done],
                             group="w1r", tag=f"w1r{j}")
        f2bar = [prj_stores[-1], w1r_last, ex_last, av_last,
                 last_wave, oiq_toks[-1][1], sc_last]
        f2 = emit_ffn(xs2v, dyv, f2bar, "F2", prj_stores[-1], w2_ready)

        p.barrier_tokens(f2["stores"])
        p.op("sync", lambda s: None, deps=f2["stores"], tag="fin")

        sems = {nm: ee(nc.semaphore("s_" + nm)) for nm in p.sem_names()}
        p.build(block, sems)

    nc.compile()
    return nc


_NC_CACHE = None


def kernel(**inputs):
    global _NC_CACHE
    x = np.asarray(inputs["L0_attn_x"], np.float32)
    wqkv = np.asarray(inputs["L0_attn_W_qkv"], np.float32)
    bqkv = np.asarray(inputs["L0_attn_b_qkv"], np.float32)
    wp = np.asarray(inputs["L0_attn_W_proj"], np.float32)
    bp = np.asarray(inputs["L0_attn_b_proj"], np.float32)
    sw = np.asarray(inputs["L0_attn_s_w_qkv"], np.float32)
    zw = np.asarray(inputs["L0_attn_zp_w_qkv"], np.float32)
    swp = np.asarray(inputs["L0_attn_s_w_proj"], np.float32)
    zwp = np.asarray(inputs["L0_attn_zp_w_proj"], np.float32)
    w1 = np.asarray(inputs["L0_ffn_w1"], np.float32)
    b1 = np.asarray(inputs["L0_ffn_b1"], np.float32)
    w2 = np.asarray(inputs["L0_ffn_w2"], np.float32)
    b2 = np.asarray(inputs["L0_ffn_b2"], np.float32)
    s_aq = float(np.asarray(inputs["L0_attn_s_act_qkv"]))
    zp_aq = float(np.asarray(inputs["L0_attn_zp_act_qkv"]))
    s_at = float(np.asarray(inputs["L0_attn_s_attn"]))
    zp_at = float(np.asarray(inputs["L0_attn_zp_attn"]))
    s_ap = float(np.asarray(inputs["L0_attn_s_act_proj"]))
    zp_ap = float(np.asarray(inputs["L0_attn_zp_act_proj"]))
    assert zp_aq == 128.0 and zp_at == 128.0 and zp_ap == 128.0, (
        "kernel assumes zero points of 128 (as produced by setup_inputs)")

    if _NC_CACHE is None:
        _NC_CACHE = build_nc()
    nc = _NC_CACHE

    common = {
        "w1t": np.ascontiguousarray(w1.T),
        "w2t": np.ascontiguousarray(w2.T),
        "wqkvT": np.ascontiguousarray(wqkv.T),
        "wpT": np.ascontiguousarray(wp.T),
        "bqkv": np.ascontiguousarray(bqkv.reshape(24, P).T),
        "bp": np.ascontiguousarray(bp.reshape(8, P).T),
        "b1": np.ascontiguousarray(b1.reshape(32, P).T),
        "b2": np.ascontiguousarray(b2.reshape(8, P).T),
        "sw": np.ascontiguousarray(sw.reshape(24, P).T),
        "swp": np.ascontiguousarray(swp.reshape(8, P).T),
        "swrow": sw.reshape(1, 3 * D),
        "zwrow": zw.reshape(1, 3 * D),
        "swprow": swp.reshape(1, D),
        "zwprow": zwp.reshape(1, D),
        "bqrow": bqkv.reshape(1, 3 * D),
        "scal": np.array([[s_aq, s_at, s_ap]], np.float32),
    }
    in_maps = []
    for c in range(8):
        b, h = c // 2, c % 2
        xc = np.ascontiguousarray(x[b, h * TOK:(h + 1) * TOK, :].T)
        m = dict(common)
        m["x"] = xc
        in_maps.append(m)

    res = run_bass_kernel_spmd(nc, in_maps, core_ids=list(range(8)))
    out = np.empty((4, 4096, D), np.float32)
    for c in range(8):
        b, h = c // 2, c % 2
        out[b, h * TOK:(h + 1) * TOK, :] = np.asarray(res.results[c]["y"]).T
    return out


# version 05
# speedup vs baseline: 1.0191x; 1.0191x over previous
import numpy as np
import ml_dtypes

import concourse.bass as bass
import concourse.bacc as bacc
import concourse.mybir as mybir
from concourse.bass_utils import run_bass_kernel_spmd
from contextlib import ExitStack
from concourse import dve_ops as _dve_ops
K_NR = _dve_ops.RECIPROCAL_APPROX_NR

F32 = mybir.dt.float32
F16 = mybir.dt.float16
FP8 = mybir.dt.float8e4
I8 = mybir.dt.int8
U8 = mybir.dt.uint8
ALU = mybir.AluOpType
ACTF = mybir.ActivationFunctionType

P = 128
TOK = 2048
TB = 512
NTB = TOK // TB
D = 1024
DC = D // P
HFF = 4096
HC = HFF // P
SFULL = 4096
NTC = SFULL // P
QB = 512
NQB = TOK // QB
NHP = 8

ENGINES = ("sync", "vector", "scalar", "tensor", "gpsimd")


class TokN:
    __slots__ = ("kind", "name", "idx", "val")

    def __init__(self, kind, name, idx, val):
        self.kind, self.name, self.idx, self.val = kind, name, idx, val


class OpN:
    __slots__ = ("engine", "emit", "deps", "is_dma", "group", "tok",
                 "inc_needed", "tag")

    def __init__(self, engine, emit, deps, is_dma, group, tok, tag):
        self.engine, self.emit, self.deps = engine, emit, deps
        self.is_dma, self.group, self.tok, self.tag = is_dma, group, tok, tag
        self.inc_needed = False


class Prog:

    def __init__(self):
        self.ops = []
        self.count = {e: 0 for e in ENGINES}
        self.dma_groups = {}

    def op(self, engine, emit, deps=(), tag=""):
        deps = [d for d in deps if d is not None]
        for d in deps:
            assert d.idx < len(self.ops), f"dep not earlier: {tag}"
        self.count[engine] += 1
        tok = TokN("op", engine, len(self.ops), self.count[engine])
        self.ops.append(OpN(engine, emit, deps, False, None, tok, tag))
        return tok

    def dma(self, queue, out, in_, deps=(), group="g0", tag=""):
        deps = [d for d in deps if d is not None]
        for d in deps:
            assert d.idx < len(self.ops), f"dep not earlier: {tag}"
        self.dma_groups[group] = self.dma_groups.get(group, 0) + 1
        tok = TokN("dma", group, len(self.ops), 16 * self.dma_groups[group])

        def emit(eng, out=out, in_=in_):
            return eng.dma_start(out, in_)

        self.ops.append(OpN(queue, emit, deps, True, group, tok, tag))
        return tok

    def barrier_tokens(self, toks):
        for t in toks:
            if t is not None:
                self.ops[t.idx].inc_needed = True

    def sem_names(self):
        return (["op_" + e for e in ENGINES if self.count[e]]
                + ["dma_" + g for g in self.dma_groups])

    def build(self, block, sems):
        for o in self.ops:
            for d in o.deps:
                self.ops[d.idx].inc_needed = True
        grp_needed = {o.group for o in self.ops if o.is_dma and o.inc_needed}
        cnt = {}
        for o in self.ops:
            if o.is_dma:
                if o.group in grp_needed:
                    o.inc_needed = True
                    k = ("dma", o.group)
                    cnt[k] = cnt.get(k, 0) + 1
                    o.tok.val = 16 * cnt[k]
            elif o.inc_needed:
                k = ("op", o.engine)
                cnt[k] = cnt.get(k, 0) + 1
                o.tok.val = cnt[k]
        per_engine = {e: [] for e in ENGINES}
        for o in self.ops:
            per_engine[o.engine].append(o)
        watermark = {e: {} for e in ENGINES}
        emit_plan = {e: [] for e in ENGINES}
        for e in ENGINES:
            for o in per_engine[e]:
                merged = {}
                for d in o.deps:
                    sn = ("dma_" + d.name) if d.kind == "dma" else ("op_" + d.name)
                    if d.val > watermark[e].get(sn, -1):
                        merged[sn] = max(merged.get(sn, 0), d.val)
                        watermark[e][sn] = d.val
                emit_plan[e].append((sorted(merged.items()), o))

        def mk_body(e):
            def body(eng):
                for waits, o in emit_plan[e]:
                    for sn, val in waits:
                        eng.wait_ge(sems[sn], val)
                    try:
                        inst = o.emit(eng)
                    except Exception as ex:
                        raise RuntimeError(f"emit failed at {o.tag}: {ex}") from ex
                    if o.inc_needed:
                        assert inst is not None, f"inc-needed {o.tag} gave None"
                        sn = ("dma_" + o.group) if o.is_dma else ("op_" + o.engine)
                        inst.then_inc(sems[sn], 16 if o.is_dma else 1)

            return body

        for e in ENGINES:
            if per_engine[e]:
                getattr(block, e)(mk_body(e))


class Arena:
    def __init__(self, handle, nbytes):
        self.h, self.nbytes = handle, nbytes
        self._cast = {}

    def _bt(self, dtype):
        if dtype not in self._cast:
            self._cast[dtype] = self.h if dtype == U8 else self.h.bitcast(dtype)
        return self._cast[dtype]

    def view(self, off_bytes, dtype, rows, cols, row0=0):
        esz = mybir.dt.size(dtype)
        assert off_bytes % esz == 0, (off_bytes, dtype)
        assert off_bytes + cols * esz <= self.nbytes, (off_bytes, cols, dtype)
        assert row0 + rows <= P
        bt = self._bt(dtype)
        re = self.nbytes // esz
        return bass.AP(bt, row0 * re + off_bytes // esz,
                       [[re, rows], [1, cols]])

    def bview(self, off_bytes, dtype, rows, cols, row0=0):
        esz = mybir.dt.size(dtype)
        bt = self._bt(dtype)
        re = self.nbytes // esz
        return bass.AP(bt, row0 * re + off_bytes // esz,
                       [[re, 1], [0, rows], [1, cols]])

    def view3(self, off_bytes, dtype, rows, shape2, strides2, row0=0):
        esz = mybir.dt.size(dtype)
        bt = self._bt(dtype)
        re = self.nbytes // esz
        return bass.AP(bt, row0 * re + off_bytes // esz,
                       [[re, rows]] + [[s, n] for s, n in
                                       zip(strides2, shape2)])


def build_nc():
    nc = bacc.Bacc("TRN2", target_bir_lowering=False)

    dxin = nc.dram_tensor("x", [D, TOK], F32, kind="ExternalInput").ap()
    dw1t = nc.dram_tensor("w1t", [D, HFF], F32, kind="ExternalInput").ap()
    dw2t = nc.dram_tensor("w2t", [HFF, D], F32, kind="ExternalInput").ap()
    dwqkvT = nc.dram_tensor("wqkvT", [D, 3 * D], F32,
                            kind="ExternalInput").ap()
    dwpT = nc.dram_tensor("wpT", [D, D], F32, kind="ExternalInput").ap()
    dbqkv = nc.dram_tensor("bqkv", [P, 24], F32, kind="ExternalInput").ap()
    dbp = nc.dram_tensor("bp", [P, 8], F32, kind="ExternalInput").ap()
    db1 = nc.dram_tensor("b1", [P, 32], F32, kind="ExternalInput").ap()
    db2 = nc.dram_tensor("b2", [P, 8], F32, kind="ExternalInput").ap()
    dsw = nc.dram_tensor("sw", [P, 24], F32, kind="ExternalInput").ap()
    dswp = nc.dram_tensor("swp", [P, 8], F32, kind="ExternalInput").ap()
    dswrow = nc.dram_tensor("swrow", [1, 3 * D], F32,
                            kind="ExternalInput").ap()
    dzwrow = nc.dram_tensor("zwrow", [1, 3 * D], F32,
                            kind="ExternalInput").ap()
    dswprow = nc.dram_tensor("swprow", [1, D], F32,
                             kind="ExternalInput").ap()
    dzwprow = nc.dram_tensor("zwprow", [1, D], F32,
                             kind="ExternalInput").ap()
    dbqrow = nc.dram_tensor("bqrow", [1, 3 * D], F32, kind="ExternalInput").ap()
    dscal = nc.dram_tensor("scal", [1, 3], F32, kind="ExternalInput").ap()
    dy = nc.dram_tensor("y", [D, TOK], F32, kind="ExternalOutput").ap()

    xs1 = nc.dram_tensor("xs1", [DC, P, TOK], F32).ap()
    xs2 = nc.dram_tensor("xs2", [DC, P, TOK], F32).ap()
    kscr = nc.dram_tensor("kscr", [DC, P, SFULL], F16).ap()
    w1scr = nc.dram_tensor("w1scr", [DC, P, HFF], F16).ap()
    w2scr = nc.dram_tensor("w2scr", [HC, P, D], FP8).ap()

    with ExitStack() as es:
        ee = es.enter_context
        AR = 212832
        art = ee(nc.sbuf_tensor("arena", [P, AR], U8))
        ar = Arena(art, AR)

        O_SCAL = 0
        O_SROW = 864
        O_BC = 2912
        O_XIO = 7008
        O_XQ = 23392
        O_HQ = O_XQ + 8192
        O_XQF = O_XQ + 16384
        O_XR = O_XQ + 24576
        O_Q = O_XR + 66560
        O_WQT = O_Q + 32768
        O_WPT = O_WQT + 49152
        AR_END = O_WPT + 16384
        assert AR_END <= AR, AR_END

        ones32 = ar.view(0, F32, P, 1)
        v_sa = ar.view(4, F32, P, 1)
        v_sattn = ar.view(8, F32, P, 1)
        v_c8 = ar.view(12, F32, P, 1)
        v_r50 = ar.view(16, F32, P, 1)
        v_sw = ar.view(20, F32, P, 24)
        v_scq = ar.view(308, F32, P, 24)
        v_bq = ar.view(404, F32, P, 24)
        v_swp = ar.view(500, F32, P, 8)
        v_scp = ar.view(596, F32, P, 8)
        v_bp = ar.view(628, F32, P, 8)
        v_b1 = ar.view(660, F32, P, 32)
        v_b2q = ar.view(788, F32, P, 8)
        rdys = ar.view(820, F16, P, 4)
        rdyd = ar.view(828, F16, P, 4)
        scal_t = ar.view(836, F32, 1, 3)
        scal2 = ar.view(848, F32, 1, 4)
        ones16 = rdys[0:P, 0:1]
        eps_t = scal2[0:1, 3:4]

        def srow0(cols=TB, row0=0):
            return ar.view(O_SROW, F32, 1, cols, row0=row0)

        vrow = ar.view(O_SROW, F32, 1, TB, row0=0)

        rs_b = ar.view(O_BC, F32, P, TB)
        mrs_b = ar.view(O_BC + 2048, F32, P, TB)
        zbA = ar.view(O_BC, F32, 64, QB)
        zbB = ar.view(O_BC + 2048, F32, 64, QB)
        kstg = [ar.view(O_BC, F16, P, TB), ar.view(O_BC + 1024, F16, P, TB)]
        vrow2 = ar.view(O_BC + 2048, F32, 1, TB)

        def xio(j):
            return ar.view(O_XIO + j * 2048, F32, P, TB)

        wstg = [ar.view(O_XIO, F32, P, 2048),
                ar.view(O_XIO + 8192, F32, P, 2048)]
        wstg4 = [ar.view(O_XIO, F32, P, HFF),
                 ar.view(O_XQ, F32, P, HFF)]

        def xlnv(j):
            return ar.view(O_XQ + j * 1024, F16, P, TB)

        def xqi8(j):
            return ar.view(O_XQ + j * 2048, I8, P, TOK)

        i8stv = [ar.view(O_XQ, I8, P, 2048),
                 ar.view(O_XQ + 2048, I8, P, 2048)]
        estv = [ar.view(O_XQ + 4096, F16, P, 2048),
                ar.view(O_XIO + 8192, F16, P, 2048)]
        vev2 = [ar.view(O_WPT + 7168, F32, P, TB),
                ar.view(O_WPT + 9216, F32, P, TB)]
        vev3 = [ar.view(O_XIO + 8192, F32, P, TB),
                ar.view(O_XIO + 10240, F32, P, TB)]

        def hq_v(c):
            return ar.view(O_HQ + c * TB, FP8, P, TB)

        def oq16(jf):
            return ar.view(O_HQ + 1024 * jf, F16, P, QB)

        vsc_b = [ar.view(O_XIO, F32, P, TB),
                 ar.view(O_XIO + 2048, F32, P, TB)]
        vbc_b = [ar.view(O_XIO + 4096, F32, P, TB),
                 ar.view(O_XIO + 6144, F32, P, TB)]

        def xqf(j):
            return ar.view(O_XQF + j * 1024, F16, P, TB)


        def w1t(j):
            return ar.view(O_XR + j * 8192, F16, P, HFF)

        def vtile(tc):
            return ar.view(O_XR + tc * 2080, F16, P, 1040)

        def vslice(tc, h):
            return ar.view(O_XR + tc * 2080 + 65 * h * 2, F16, P, 65)

        def vones(tc):
            return ar.view3(O_XR + tc * 2080 + 128, F16, P, [16], [65])

        def vblock(tc, vb):
            return ar.view3(O_XR + tc * 2080 + 65 * 8 * vb * 2,
                            F16, P, [8, 64], [65, 1])

        def qhp(hp):
            return ar.view(O_Q + hp * 4096, F16, P, TOK)

        w2buf = [ar.view(O_WPT + 5120, FP8, P, D),
                 ar.view(O_WPT + 6144, FP8, P, D)]

        def wqT(j):
            return ar.view(O_WQT + j * 6144, F16, P, 3 * D)

        def krecv(j):
            return ar.view(O_WQT + j * 4096, F16, P, TOK)

        def khp(par):
            return ar.view(O_HQ + par * 4096, F16, P, TOK)

        def outiq(j):
            return ar.view(O_WQT + 32 * 1024 + j * 2048, I8, P, TOK)

        def wpT(j):
            return ar.view(O_WPT + j * 2048, F16, P, D)

        sqtmp = ar.view(O_WPT, F16, P, TB)
        t16b = [ar.view(O_WPT + 1024, F16, P, TB),
                ar.view(O_WPT + 2048, F16, P, TB)]

        pa = ee(nc.psum_tensor("pa", [P, 4096], F32))

        def pbk(bank, cols=512, rows=P, row0=0, col0=0):
            return bass.AP(pa, row0 * 4096 + bank * 512 + col0,
                           [[4096, rows], [1, cols]])

        rsem_r = ee(nc.semaphore("rsem_r"))
        rsem_rv = ee(nc.semaphore("rsem_rv"))
        rsem_k = ee(nc.semaphore("rsem_k"))
        rsem_v = ee(nc.semaphore("rsem_v"))
        lsem = ee(nc.semaphore("lsem"))
        lsem2 = ee(nc.semaphore("lsem2"))
        lsemv = ee(nc.semaphore("lsemv"))

        block = ee(nc.Block())
        p = Prog()
        RD = [(0, 1)] + [None] * 7

        g_iv = "iv"
        for dst, src in [(scal_t, dscal), (v_sw, dsw),
                         (v_swp, dswp), (v_bq, dbqkv),
                         (v_bp, dbp), (v_b1, db1), (v_b2q, db2)]:
            iv_last = p.dma("sync", dst, src, group=g_iv, tag="iv")

        c_o32 = p.op("gpsimd", lambda g: g.memset(ones32, 1.0), tag="o32")
        c_eps = p.op("gpsimd", lambda g: g.memset(eps_t, 1e-5), tag="eps")
        c_rdy = p.op("gpsimd", lambda g: g.memset(rdys, 1.0), tag="rdys")
        c_o16 = c_rdy

        s_d1 = p.op("vector", lambda v: v.tensor_scalar(
            scal2[0:1, 0:1], scal_t[0:1, 1:2], 8.0, None, ALU.mult),
            deps=[iv_last], tag="d1")
        s_d2 = p.op("vector", lambda v: v.reciprocal(
            scal2[0:1, 0:1], scal2[0:1, 0:1]), deps=[s_d1], tag="d2")
        s_d3 = p.op("vector", lambda v: v.reciprocal(
            scal2[0:1, 1:2], scal_t[0:1, 2:3]), deps=[iv_last], tag="d3")
        s_d4 = p.op("vector", lambda v: v.reciprocal(
            scal2[0:1, 2:3], scal_t[0:1, 0:1]), deps=[iv_last], tag="d4")
        b_sa = p.dma("sync", v_sa, ar.bview(836, F32, P, 1),
                     deps=[iv_last], group="cb1", tag="b_sa")
        b_st = p.dma("sync", v_sattn, ar.bview(840, F32, P, 1),
                     deps=[iv_last], group="cb2", tag="b_st")
        b_c8 = p.dma("sync", v_c8, ar.bview(848, F32, P, 1),
                     deps=[s_d2], group="cb3", tag="b_c8")
        b_r50 = p.dma("sync", v_r50, ar.bview(852, F32, P, 1),
                      deps=[s_d3], group="cb4", tag="b_r50")
        b_sap = p.dma("sync", v_scp[:, 0:1], ar.bview(844, F32, P, 1),
                      deps=[iv_last], group="cb5", tag="b_sap")

        s_scq = p.op("vector", lambda v: v.tensor_scalar(
            v_scq, v_sw, v_sa, None, ALU.mult), deps=[iv_last, b_sa],
            tag="scq")
        s_scq2 = p.op("vector", lambda v: v.tensor_scalar(
            v_scq[:, 0:8], v_scq[:, 0:8], v_c8, None, ALU.mult),
            deps=[s_scq, b_c8], tag="scq2")
        s_bq2 = p.op("vector", lambda v: v.tensor_scalar(
            v_bq[:, 0:8], v_bq[:, 0:8], v_c8, None, ALU.mult),
            deps=[iv_last, b_c8], tag="bq2")
        s_scp_a = p.op("vector", lambda v: v.tensor_scalar(
            v_scp[:, 1:8], v_swp[:, 1:8], v_scp[:, 0:1], None, ALU.mult),
            deps=[iv_last, b_sap], tag="scp_a")
        s_scp = p.op("vector", lambda v: v.tensor_tensor(
            v_scp[:, 0:1], v_swp[:, 0:1], v_scp[:, 0:1], ALU.mult),
            deps=[s_scp_a], tag="scp")
        s_b2a = p.op("vector", lambda v: v.tensor_copy(sqtmp[:, 0:8], v_b2q),
                     deps=[iv_last], tag="b2a")
        s_b2b = p.op("vector", lambda v: v.tensor_copy(v_b2q, sqtmp[:, 0:8]),
                     deps=[s_b2a], tag="b2b")

        last_c = [None, None]
        w1s_last = None
        for j in range(DC):
            par = j % 2
            ld = p.dma("sync", wstg4[par], dw1t[j * P:(j + 1) * P, :],
                       deps=[last_c[par]], group=f"wl{par}", tag=f"w1l{j}")
            cv = p.op("vector", lambda v, j=j, par=par: v.tensor_copy(
                w1t(j), wstg4[par]), deps=[ld], tag=f"w1c{j}")
            last_c[par] = cv
            w1s_last = p.dma("gpsimd", w1scr[j], w1t(j), deps=[cv],
                             group="w1s", tag=f"w1s{j}")
        w1_ready = [last_c[0], last_c[1]]

        last_c2 = [None, None]
        for c in range(HC):
            par = c % 2
            ld = p.dma("sync", wstg[par][:, 0:D], dw2t[c * P:(c + 1) * P, :],
                       deps=[last_c2[par], w1_ready[par]],
                       group=f"w2l{par}", tag=f"w2l{c}")
            cv = p.op("vector", lambda v, c=c, par=par: v.tensor_copy(
                w2buf[par], wstg[par][:, 0:D]), deps=[ld], tag=f"w2c{c}")
            stw = p.dma("gpsimd", w2scr[c], w2buf[par], deps=[cv],
                        group=f"w2s{par}", tag=f"w2s{c}")
            last_c2[par] = stw
        w2_ready = [last_c2[0], last_c2[1]]
        w2s_last = None

        def emit_ffn(xsrc, xdst, bar, phase, extra_load_dep, w2_prepped):
            store_toks = []
            prev_store = None
            info = {}
            pv = {}
            for tb in range(NTB):
                sl = slice(tb * TB, (tb + 1) * TB)
                bar0 = bar if tb == 0 else []
                lds = []
                for j in range(DC):
                    lds.append(p.dma(
                        "sync", xio(j), xsrc[j][:, sl],
                        deps=([prev_store] if prev_store else [])
                        + bar0 + ([extra_load_dep] if tb == 0 else []),
                        group=f"fx{phase}", tag=f"{phase}xl{tb}_{j}"))
                ldall = lds[-1]
                mm_sx = None
                for j in range(DC):
                    mm_sx = p.op("tensor", lambda t, j=j: t.matmul(
                        pbk(4, TB, 1, 0), ones32, xio(j),
                        start=(j == 0), stop=(j == DC - 1)),
                        deps=[ldall, c_o32, mm_sx] + bar0
                        + ([pv.get("mean"), pv.get("res")] if j == 0 else []),
                        tag=f"{phase}sx{tb}_{j}")
                mm_sq = pv.get("sq")
                for j in range(DC):
                    sq = p.op("vector", lambda v, j=j: v.tensor_mul(
                        sqtmp, xio(j), xio(j)),
                        deps=[ldall, mm_sq] + bar0, tag=f"{phase}sq{tb}_{j}")
                    mm_sq = p.op("tensor", lambda t, j=j: t.matmul(
                        pbk(5, TB, 1, 0), ones16, sqtmp,
                        start=(j == 0), stop=(j == DC - 1)),
                        deps=[sq, c_o16, mm_sx]
                        + ([pv.get("var")] if j == 0 else []),
                        tag=f"{phase}sq2{tb}_{j}")
                st_mean = p.op("vector", lambda v: v.tensor_scalar(
                    srow0(), pbk(4, TB, 1, 0), 1.0 / D, None, ALU.mult),
                    deps=[mm_sx, pv.get("mrs")],
                    tag=f"{phase}mean{tb}")
                st_m2 = p.op("vector", lambda v: v.tensor_mul(
                    rs_b[0:1, :], srow0(), srow0()),
                    deps=[st_mean, pv.get("bc_rs"), pv.get("ap")],
                    tag=f"{phase}m2{tb}")
                st_var = p.op("vector", lambda v: v.scalar_tensor_tensor(
                    rs_b[0:1, :], pbk(5, TB, 1, 0), 1.0 / D, rs_b[0:1, :],
                    ALU.mult, ALU.subtract), deps=[mm_sq, st_m2],
                    tag=f"{phase}var{tb}")
                st_ln = p.op("scalar", lambda s: s.activation(
                    mrs_b[0:1, :], rs_b[0:1, :], ACTF.Ln, bias=eps_t,
                    scale=1.0),
                    deps=[st_var, c_eps, pv.get("bc_mrs"), pv.get("ap")]
                    + bar0,
                    tag=f"{phase}lnv{tb}")
                st_rs = p.op("scalar", lambda s: s.activation(
                    mrs_b[0:1, :], mrs_b[0:1, :], ACTF.Exp, bias=0.0,
                    scale=-0.5),
                    deps=[st_ln], tag=f"{phase}rs{tb}")
                st_mrs = p.op("vector", lambda v: v.scalar_tensor_tensor(
                    srow0(), srow0(), -1.0, mrs_b[0:1, :],
                    ALU.mult, ALU.mult),
                    deps=[st_rs, st_mean], tag=f"{phase}mrs{tb}")
                bc_rs = p.dma("sync", rs_b,
                              ar.bview(O_BC + 2048, F32, P, TB),
                              deps=[st_rs, st_var, pv.get("ap")] + bar0,
                              group=f"brs{phase}", tag=f"{phase}brs{tb}")
                bc_mrs = p.dma("gpsimd", mrs_b,
                               ar.bview(O_SROW, F32, P, TB),
                               deps=[st_mrs, bc_rs],
                               group=f"bmrs{phase}", tag=f"{phase}bmrs{tb}")
                ap_toks = []
                for j in range(DC):
                    par = j % 2
                    a1 = p.op("vector", lambda v, j=j, par=par: v.tensor_mul(
                        t16b[par], xio(j), rs_b),
                        deps=[ldall, bc_rs]
                        + ([ap_toks[j - 2]] if j >= 2 else []),
                        tag=f"{phase}ap1{tb}_{j}")
                    a2 = p.op("vector", lambda v, j=j, par=par:
                              v.tensor_tensor(xlnv(j), t16b[par], mrs_b,
                                              ALU.add),
                              deps=[a1, bc_mrs] + bar0,
                              tag=f"{phase}ap2{tb}_{j}")
                    ap_toks.append(a2)
                hq_toks = []
                last_fa = None
                for c in range(HC):
                    par = c % 2
                    mm = None
                    for j in range(DC):
                        mm = p.op(
                            "tensor", lambda t, c=c, j=j, par=par: t.matmul(
                                pbk(par, TB), w1t(j)[:, c * P:(c + 1) * P],
                                xlnv(j), start=(j == 0), stop=(j == DC - 1)),
                            deps=[ap_toks[j], mm]
                            + ([hq_toks[c - 2]] if c >= 2 else [])
                            + (bar0 + [pv.get("res")]
                               if c == 0 and j == 0 else []),
                            tag=f"{phase}fa{tb}_{c}_{j}")
                    last_fa = mm
                    hq_toks.append(p.op(
                        "scalar", lambda s, c=c, par=par: s.activation(
                            hq_v(c), pbk(par, TB), ACTF.Relu,
                            bias=v_b1[:, c:c + 1], scale=1.0),
                        deps=[mm] + (bar0 if c == 0 else []),
                        tag=f"{phase}hq{tb}_{c}"))
                res_toks = []
                w2l_last = [pv.get("w2a"), pv.get("w2b")]
                mms = [None] * DC
                for c in range(HC):
                    par = c % 2
                    w2l = p.dma("sync", w2buf[par], w2scr[c],
                                deps=[w2l_last[par]] + list(w2_prepped)
                                + ([hq_toks[1]] if c < 2 else []),
                                group=f"w2f{phase}{par}", tag=f"{phase}w2l{tb}_{c}")
                    for jd in range(DC):
                        mms[jd] = p.op(
                            "tensor", lambda t, jd=jd, c=c, par=par: t.matmul(
                                pbk(jd, TB),
                                w2buf[par][:, jd * P:(jd + 1) * P], hq_v(c),
                                start=(c == 0), stop=(c == HC - 1)),
                            deps=[w2l, hq_toks[c], mms[jd]]
                            + ([hq_toks[-1], st_mean, st_var,
                                pv.get("res")] if c == 0 and jd == 0 else []),
                            tag=f"{phase}fb{tb}_{c}_{jd}")
                    w2l_last[par] = mms[DC - 1]
                for jd in range(DC):
                    par = jd % 2
                    t1 = p.op("vector", lambda v, jd=jd, par=par:
                              v.tensor_scalar(t16b[par],
                                              pbk(jd, TB),
                                              v_b2q[:, jd:jd + 1], None,
                                              ALU.add),
                              deps=[mms[jd], ap_toks[-1]]
                              + ([res_toks[jd - 2]] if jd >= 2 else []),
                              tag=f"{phase}q16{tb}_{jd}")
                    t2 = p.op("vector", lambda v, jd=jd, par=par:
                              v.scalar_tensor_tensor(
                                  xio(jd), t16b[par], 0.5, xio(jd),
                                  ALU.mult, ALU.add),
                              deps=[t1], tag=f"{phase}res{tb}_{jd}")
                    res_toks.append(t2)
                    store_toks.append(p.dma(
                        "gpsimd", xdst[jd][:, sl], xio(jd), deps=[t2],
                        group=f"fs{phase}", tag=f"{phase}xst{tb}_{jd}"))
                prev_store = store_toks[-1]
                pv = dict(mean=st_mean, m2=st_m2, var=st_var, mrs=st_mrs,
                          bc_rs=bc_rs, bc_mrs=bc_mrs, ap=ap_toks[-1],
                          sq=mm_sq, res=res_toks[-1],
                          w2a=w2l_last[0], w2b=w2l_last[1])
                info = dict(last_fa=last_fa, last_res=res_toks[-1],
                            last_hq=hq_toks[-1], last_rs=st_rs,
                            last_bc=bc_mrs, last_sq=mm_sq)
            info["stores"] = store_toks
            return info

        xs1v = [xs1[j] for j in range(DC)]
        xs2v = [xs2[j] for j in range(DC)]
        dxv = [dxin[j * P:(j + 1) * P, :] for j in range(DC)]
        dyv = [dy[j * P:(j + 1) * P, :] for j in range(DC)]

        f1bar = [w1_ready[0], w1_ready[1], w2_ready[0], w2_ready[1],
                 iv_last, s_b2b, c_o32, c_o16, c_eps]
        f1 = emit_ffn(dxv, xs1v, f1bar, "F1", None, w2_ready)

        ph2bar = [f1["stores"][-1], f1["last_fa"], f1["last_res"],
                  f1["last_rs"], f1["last_bc"], f1["last_hq"]]
        xq_last = None
        prev = None
        last_a1 = [None, None]
        pv2 = {"mean": f1["last_bc"], "mrs": f1["last_bc"],
               "ap": f1["last_res"], "bc_rs": f1["last_bc"],
               "bc_mrs": f1["last_bc"], "sq": f1["last_sq"]}
        for tb in range(NTB):
            sl = slice(tb * TB, (tb + 1) * TB)
            bar0 = ph2bar if tb == 0 else []
            lds = []
            for j in range(DC):
                lds.append(p.dma("sync", xio(j), xs1[j][:, sl],
                                 deps=([prev] if prev else [])
                                 + ([f1["stores"][-1]] if tb == 0 else []),
                                 group="p2x", tag=f"p2xl{tb}_{j}"))
            ldall = lds[-1]
            mm_sx = None
            for j in range(DC):
                mm_sx = p.op("tensor", lambda t, j=j: t.matmul(
                    pbk(4, TB, 1, 0), ones32, xio(j),
                    start=(j == 0), stop=(j == DC - 1)),
                    deps=[ldall, mm_sx] + bar0
                    + ([pv2.get("mean")] if j == 0 else []),
                    tag=f"p2sx{tb}_{j}")
            mm_sq = pv2.get("sq")
            for j in range(DC):
                sq = p.op("vector", lambda v, j=j: v.tensor_mul(
                    sqtmp, xio(j), xio(j)),
                    deps=[ldall, mm_sq] + bar0, tag=f"p2sq{tb}_{j}")
                mm_sq = p.op("tensor", lambda t, j=j: t.matmul(
                    pbk(5, TB, 1, 0), ones16, sqtmp,
                    start=(j == 0), stop=(j == DC - 1)),
                    deps=[sq, mm_sx]
                    + ([pv2.get("var")] if j == 0 else []),
                    tag=f"p2sq2{tb}_{j}")
            st_mean = p.op("vector", lambda v: v.tensor_scalar(
                srow0(), pbk(4, TB, 1, 0), 1.0 / D, None, ALU.mult),
                deps=[mm_sx, pv2.get("mrs")],
                tag=f"p2mean{tb}")
            st_m2 = p.op("vector", lambda v: v.tensor_mul(
                rs_b[0:1, :], srow0(), srow0()),
                deps=[st_mean, pv2.get("bc_rs"), pv2.get("ap")],
                tag=f"p2m2{tb}")
            st_var = p.op("vector", lambda v: v.scalar_tensor_tensor(
                rs_b[0:1, :], pbk(5, TB, 1, 0), 1.0 / D, rs_b[0:1, :],
                ALU.mult, ALU.subtract), deps=[mm_sq, st_m2],
                tag=f"p2var{tb}")
            st_ln = p.op("scalar", lambda s: s.activation(
                mrs_b[0:1, :], rs_b[0:1, :], ACTF.Ln, bias=eps_t, scale=1.0),
                deps=[st_var, pv2.get("bc_mrs"), pv2.get("ap")] + bar0,
                tag=f"p2lnv{tb}")
            st_rs = p.op("scalar", lambda s: s.activation(
                mrs_b[0:1, :], mrs_b[0:1, :], ACTF.Exp, bias=0.0,
                scale=-0.5),
                deps=[st_ln], tag=f"p2rs{tb}")
            st_rs2 = p.op("vector", lambda v: v.tensor_scalar(
                mrs_b[0:1, :], mrs_b[0:1, :], scal2[0:1, 2:3], None,
                ALU.mult),
                deps=[st_rs, s_d4], tag=f"p2rs2{tb}")
            st_mrs = p.op("vector", lambda v: v.scalar_tensor_tensor(
                srow0(), srow0(), -1.0, mrs_b[0:1, :], ALU.mult, ALU.mult),
                deps=[st_rs2, st_mean], tag=f"p2mrs{tb}")
            bc_rs = p.dma("sync", rs_b,
                          ar.bview(O_BC + 2048, F32, P, TB),
                          deps=[st_rs2, st_var, pv2.get("ap")] + bar0,
                          group="p2brs", tag=f"p2brs{tb}")
            bc_mrs = p.dma("gpsimd", mrs_b,
                           ar.bview(O_SROW, F32, P, TB),
                           deps=[st_mrs, bc_rs],
                           group="p2bmrs", tag=f"p2bmrs{tb}")
            for j in range(DC):
                par = j % 2
                a1 = p.op("vector", lambda v, j=j, par=par: v.tensor_mul(
                    vev2[par], xio(j), rs_b),
                    deps=[ldall, bc_rs, last_a1[par]], tag=f"p2a1{tb}_{j}")
                a2 = p.op("vector", lambda v, j=j, par=par, sl=sl:
                          v.tensor_tensor(xqi8(j)[:, sl], vev2[par], mrs_b,
                                          ALU.add),
                          deps=[a1, bc_mrs] + bar0, tag=f"p2a2{tb}_{j}")
                last_a1[par] = a2
                xq_last = a2
            prev = xq_last
            pv2 = dict(mean=st_mean, m2=st_m2, var=st_var, mrs=st_mrs,
                       bc_rs=bc_rs, bc_mrs=bc_mrs, ap=xq_last, sq=mm_sq)

        zwB = ar.view(O_XR, F32, P, 3 * D)
        zwpB = ar.view(O_XR + 12288, F32, P, D)
        rswB = ar.view(O_XR + 16384, F32, P, 3 * D)
        rswpB = ar.view(O_XR + 28672, F32, P, D)
        xr_free = [f1["last_fa"], w1s_last]
        rowlds = []
        for gi, (tile, off, cols, dsrc, dep) in enumerate([
                (rswB, O_XR + 16384, 3 * D, dswrow, xr_free),
                (rswpB, O_XR + 28672, D, dswprow, xr_free),
                (zwB, O_XR, 3 * D, dzwrow, xr_free),
                (zwpB, O_XR + 12288, D, dzwprow, xr_free)]):
            rl = p.dma("sync", ar.view(off, F32, 1, cols), dsrc,
                       deps=dep + [xq_last], group=f"rw{gi}",
                       tag=f"rwl{gi}")
            if gi < 2:
                rl = p.op("vector", lambda v, off=off, cols=cols:
                          v.reciprocal(ar.view(off, F32, 1, cols),
                                       ar.view(off, F32, 1, cols)),
                          deps=[rl], tag=f"rwr{gi}")
            rowlds.append(p.dma(
                "gpsimd", ar.view(off, F32, P - 1, cols, row0=1),
                ar.bview(off, F32, P - 1, cols),
                deps=[rl], group=f"rb{gi}", tag=f"rwb{gi}"))
        rows_ready = rowlds

        wstg_q = [ar.view(O_XIO, F32, P, D),
                  ar.view(O_XIO + 8192, F32, P, D)]
        u8t = [ar.view(O_XIO + 4096, U8, P, D),
               ar.view(O_XIO + 12288, U8, P, D)]
        wtmp = [ar.view(O_XQF, F32, P, D),
                ar.view(O_XQF + 4096, F32, P, D)]
        pieces = [(False, j, pc) for j in range(DC) for pc in range(3)]
        pieces += [(True, j, 0) for j in range(DC)]
        last_ld = [None, None]
        last_t1 = [None, None]
        last_t2 = [None, None]
        last_t3 = [None, None]
        for idx, (is_p, j, pc) in enumerate(pieces):
            par = idx % 2
            if is_p:
                srcd = dwpT[j * P:(j + 1) * P, :]
                rsB, zB = rswpB, zwpB
                dst = wpT(j)
            else:
                srcd = dwqkvT[j * P:(j + 1) * P, pc * D:(pc + 1) * D]
                rsB = rswB[0:P, pc * D:(pc + 1) * D]
                zB = zwB[0:P, pc * D:(pc + 1) * D]
                dst = wqT(j)[:, pc * D:(pc + 1) * D]
            ld = p.dma("sync", wstg_q[par], srcd,
                       deps=[last_t1[par]]
                       + ([xq_last] if idx < 2 else []),
                       group=f"wq{par}", tag=f"wql{idx}")
            t1 = p.op("vector", lambda v, par=par, rsB=rsB:
                      v.tensor_tensor(wtmp[par], wstg_q[par], rsB,
                                      ALU.mult),
                      deps=[ld, last_t2[par]] + rows_ready
                      + ([xq_last] if idx < 2 else []),
                      tag=f"wq1{idx}")
            t2 = p.op("vector", lambda v, par=par, zB=zB:
                      v.tensor_tensor(u8t[par], wtmp[par], zB, ALU.add),
                      deps=[t1, last_t3[par]], tag=f"wq2{idx}")
            t3 = p.op("vector", lambda v, par=par, zB=zB, dst=dst:
                      v.tensor_tensor(dst, u8t[par], zB, ALU.subtract),
                      deps=[t2], tag=f"wq3{idx}")
            last_ld[par] = ld
            last_t1[par] = t1
            last_t2[par] = t2
            last_t3[par] = t3
        wq_ready = p.op("gpsimd", lambda g: g.engine_nop(),
                        deps=[last_t3[0], last_t3[1]], tag="wqrdy")

        vrdy_snd = p.op("gpsimd", lambda g: g.remote_dma_broadcast(
            rdyd[:, 0:2], rdys[:, 0:2], rsem_rv, lsemv, rdests=RD),
            deps=[f1["last_fa"], w1s_last, c_rdy, wq_ready],
            tag="vrdy_snd")
        vrdy_trg = p.op("gpsimd", lambda g: g.trigger_dma(count=1),
                        deps=[vrdy_snd], tag="vrdy_trg")
        vrdy_wt = p.op("gpsimd", lambda g: (g.wait_ge(rsem_rv, 2),
                                            g.engine_nop())[1],
                       deps=[vrdy_trg], tag="vrdy_wt")

        vsc_toks = []
        prev_vd = None
        for vb in range(2):
            d1 = p.dma("sync", vrow,
                       dswrow[0:1, 2048 + vb * TB:2048 + (vb + 1) * TB],
                       deps=[prev_vd], group="vscd", tag=f"vswl{vb}")
            r1 = p.op("vector", lambda v: v.tensor_scalar(
                vrow2, vrow, scal_t[0:1, 0:1], None, ALU.mult),
                deps=[d1, iv_last, xq_last]
                + ([vsc_toks[0]] if vb == 1 else []), tag=f"vsc{vb}")
            b1t = p.dma("sync", vsc_b[vb],
                        ar.bview(O_BC + 2048, F32, P, TB),
                        deps=[r1, xq_last, wq_ready], group="vscb",
                        tag=f"vscb{vb}")
            d2 = p.dma("sync", vrow,
                       dbqrow[0:1, 2048 + vb * TB:2048 + (vb + 1) * TB],
                       deps=[r1], group="vscd", tag=f"vbql{vb}")
            b2t = p.dma("gpsimd", vbc_b[vb],
                        ar.bview(O_SROW, F32, P, TB),
                        deps=[d2, xq_last, wq_ready], group="vbcb",
                        tag=f"vbcb{vb}")
            prev_vd = b2t
            vsc_toks += [b1t, b2t, r1]

        kscr_toks = []
        k_store_by_jf = {}
        vtrgs = []
        q_toks = []
        vev_toks = []
        vone_toks = []
        last_kst = [None, None]
        last_mmp = [None, None]
        last_vps = [None, None]
        last_xqf = None
        last_vevb = [None, None]
        for tb in range(NTB):
            sl = slice(tb * TB, (tb + 1) * TB)
            cvs = []
            for j in range(DC):
                cvs.append(p.op(
                    "vector", lambda v, j=j, sl=sl: v.tensor_copy(
                        xqf(j), xqi8(j)[:, sl]),
                    deps=[xq_last, wq_ready]
                    + ([last_xqf] if last_xqf else []),
                    tag=f"p3cv{tb}_{j}"))
            cvall = cvs[-1]
            for which, base in (("k", 8), ("q", 0)):
                for jf in range(DC):
                    par = jf % 2
                    mm = None
                    for j in range(DC):
                        mm = p.op(
                            "tensor",
                            lambda t, jf=jf, j=j, par=par, base=base:
                            t.matmul(pbk(par, TB),
                                     wqT(j)[:, (base + jf) * P:
                                            (base + jf + 1) * P],
                                     xqf(j), start=(j == 0),
                                     stop=(j == DC - 1)),
                            deps=[cvall, mm, last_mmp[par], wq_ready],
                            tag=f"p3{which}mm{tb}_{jf}_{j}")
                    if which == "k":
                        ev = p.op(
                            "vector", lambda v, jf=jf, par=par:
                            v.tensor_scalar(kstg[par], pbk(par, TB),
                                            v_scq[:, 8 + jf:9 + jf],
                                            v_bq[:, 8 + jf:9 + jf],
                                            ALU.mult, ALU.add),
                            deps=[mm, last_kst[par], s_scq2, s_bq2, xq_last],
                            tag=f"p3kev{tb}_{jf}")
                        stw = p.dma("gpsimd", kscr[jf][:, sl],
                                    kstg[par], deps=[ev], group=f"ks{par}",
                                    tag=f"p3kst{tb}_{jf}")
                        kscr_toks.append(stw)
                        k_store_by_jf[jf] = stw
                        last_kst[par] = stw
                        last_mmp[par] = ev
                    else:
                        ev = p.op(
                            "vector", lambda v, jf=jf, par=par, sl=sl:
                            v.tensor_scalar(qhp(jf)[:, sl], pbk(par, TB),
                                            v_scq[:, jf:jf + 1],
                                            v_bq[:, jf:jf + 1],
                                            ALU.mult, ALU.add),
                            deps=[mm, s_scq2, s_bq2],
                            tag=f"p3qev{tb}_{jf}")
                        q_toks.append(ev)
                        last_mmp[par] = ev
            for ti in range(TB // P):
                tc = tb * (TB // P) + ti
                von = p.op("gpsimd", lambda g, tc=tc: g.memset(
                    vones(tc), 1.0),
                    deps=[f1["last_fa"], w1s_last, xq_last, wq_ready],
                    tag=f"p3von{tc}")
                vone_toks.append(von)
                for vb in range(2):
                    par = (ti * 2 + vb) % 2
                    mm = None
                    for j in range(DC):
                        mm = p.op(
                            "tensor",
                            lambda t, ti=ti, vb=vb, j=j, par=par: t.matmul(
                                pbk(2 + par, TB),
                                xqf(j)[:, ti * P:(ti + 1) * P],
                                wqT(j)[:, 2048 + vb * TB:
                                       2048 + (vb + 1) * TB],
                                start=(j == 0), stop=(j == DC - 1)),
                            deps=[cvall, mm, last_vps[par], wq_ready],
                            tag=f"p3vmm{tc}_{vb}_{j}")
                    e1 = p.op("vector", lambda v, par=par, vb=vb:
                              v.tensor_tensor(vev3[par], pbk(2 + par, TB),
                                              vsc_b[vb], ALU.mult),
                              deps=[mm, vsc_toks[0], vsc_toks[3],
                                    last_vevb[par]],
                              tag=f"p3ve1{tc}_{vb}")
                    e2 = p.op(
                        "vector", lambda v, tc=tc, vb=vb, par=par:
                        v.tensor_tensor(
                            vblock(tc, vb), vev3[par], vbc_b[vb], ALU.add),
                        deps=[e1, f1["last_fa"], w1s_last,
                              vsc_toks[1], vsc_toks[4]],
                        tag=f"p3ve2{tc}_{vb}")
                    vev_toks.append(e2)
                    last_vps[par] = e1
                    last_vevb[par] = e2
                vs = p.op("gpsimd", lambda g, tc=tc: g.remote_dma_broadcast(
                    vtile(16 + tc), vtile(tc), rsem_v, lsemv, rdests=RD),
                    deps=[vrdy_wt, vev_toks[-2], vev_toks[-1], von],
                    tag=f"vsnd{tc}")
                vtrgs.append(p.op("gpsimd", lambda g: g.trigger_dma(count=1),
                                  deps=[vs], tag=f"vtrg{tc}"))
            last_xqf = vev_toks[-1]
            last_cv_tok = cvall

        q_all = q_toks[-1]
        ks_par = {0: None, 1: None}
        for t_ in kscr_toks:
            ks_par[0 if t_.name == "ks0" else 1] = t_
        kscr_all = [ks_par[0], ks_par[1]]
        v_all_done = vev_toks + vone_toks

        mm_done = [last_mmp[0], last_mmp[1], last_vps[0], last_vps[1],
                   q_all, vev_toks[-1], vone_toks[-1]] + kscr_all
        rdy_snd = p.op("gpsimd", lambda g: g.remote_dma_broadcast(
            rdyd[:, 2:4], rdys[:, 2:4], rsem_r, lsemv, rdests=RD),
            deps=mm_done + [c_rdy], tag="rdy_snd")
        rdy_trg = p.op("gpsimd", lambda g: g.trigger_dma(count=1),
                       deps=[rdy_snd], tag="rdy_trg")
        rdy_wt = p.op("gpsimd", lambda g: (g.wait_ge(rsem_r, 2),
                                           g.engine_nop())[1],
                      deps=[rdy_trg], tag="rdy_wt")

        ksend2 = [ar.view(O_XIO, F16, P, TOK),
                  ar.view(O_XIO + 4096, F16, P, TOK)]
        k_lw = {}
        prev_snd = rdy_wt
        for j in range(DC):
            kb = j % 2
            ld = p.dma("sync", ksend2[kb], kscr[j][:, 0:TOK],
                       deps=[k_store_by_jf[j], k_lw.get(j - 2, rdy_wt)],
                       group=f"ksl{kb}", tag=f"ksl{j}")
            snd = p.op("gpsimd", lambda g, j=j, kb=kb:
                       g.remote_dma_broadcast(
                           krecv(j), ksend2[kb], rsem_k,
                           lsem if kb == 0 else lsem2, rdests=RD),
                       deps=[ld, prev_snd], tag=f"ksnd{j}")
            trg = p.op("gpsimd", lambda g: g.trigger_dma(count=1),
                       deps=[snd], tag=f"ktrg{j}")
            prev_snd = trg
            if j >= 1:
                jj = j - 1
                k_lw[jj] = p.op("gpsimd", lambda g, jj=jj: (
                    g.wait_ge(lsem if jj % 2 == 0 else lsem2,
                              16 * (jj // 2 + 1)), g.engine_nop())[1],
                    deps=[trg], tag=f"klw{jj}")
        jj = DC - 1
        k_lw[jj] = p.op("gpsimd", lambda g, jj=jj: (
            g.wait_ge(lsem if jj % 2 == 0 else lsem2,
                      16 * (jj // 2 + 1)), g.engine_nop())[1],
            deps=[prev_snd], tag=f"klw{jj}")
        ksend_done = p.op("gpsimd", lambda g: (
            g.wait_ge(lsem, 16 * (DC // 2)),
            g.wait_ge(lsem2, 16 * (DC // 2)), g.engine_nop())[2],
            deps=[k_lw[DC - 1], k_lw[DC - 2]], tag="ksdone")
        rkw = p.op("gpsimd", lambda g: (g.wait_ge(rsem_k, 16),
                                        g.engine_nop())[1],
                   deps=[rdy_trg], tag="rkw")
        rvw = p.op("gpsimd", lambda g: (g.wait_ge(rsem_v, 32),
                                        g.engine_nop())[1],
                   deps=[vtrgs[-1]], tag="rvw")
        vsend_done = p.op("gpsimd", lambda g: (g.wait_ge(lsemv, 288),
                                               g.engine_nop())[1],
                          deps=[vtrgs[-1], ksend_done], tag="vsend_done")

        oiq_toks = []
        last_khp_use = [None, None]
        last_oiq_par = [[None, None], [None, None]]
        av_last = None
        sc_last = None
        ex_last = None
        prev_exp_par = [None, None]
        av_par = [None, None]
        prev_zread = [None, None]
        last_oiq_any = [None, None]
        first_attn = [vev_toks[-1], vone_toks[-1]]
        for hp in range(NHP):
            par = hp % 2
            klo = p.dma("sync", khp(par), kscr[hp][:, 0:TOK],
                        deps=kscr_all + [last_khp_use[par]],
                        group=f"kho{par}", tag=f"klo{hp}")
            for qb in range(NQB):
                qsl = slice(qb * QB, (qb + 1) * QB)
                qb_first = sc_last
                avA = avB = None
                a0s = {}
                for g in range(NTC // 2):
                    tc0, tc1 = 2 * g, 2 * g + 1
                    for tc in (tc0, tc1):
                        sp = tc % 2
                        kdep = klo if tc < 16 else rkw
                        if tc < 16:
                            ksrc = khp(par)[0:P, tc * P:(tc + 1) * P]
                        else:
                            ksrc = krecv(hp)[0:P,
                                             (tc - 16) * P:(tc - 15) * P]
                        mmA = p.op(
                            "tensor", lambda t, ksrc=ksrc, sp=sp,
                            hp=hp, qsl=qsl: t.matmul(
                                pbk(2 * sp),
                                ksrc[0:64, 0:P],
                                qhp(hp)[0:64, qsl], start=True, stop=True),
                            deps=[kdep, q_all, a0s.get(tc - 2),
                                  qb_first if tc <= 1 else None]
                            + (first_attn if hp == 0 and qb == 0 and g == 0
                               else []),
                            tag=f"sA{hp}_{qb}_{tc}")
                        mmB = p.op(
                            "tensor", lambda t, ksrc=ksrc, sp=sp,
                            hp=hp, qsl=qsl: t.matmul(
                                pbk(2 * sp + 1),
                                ksrc[64:P, 0:P],
                                qhp(hp)[64:P, qsl], start=True, stop=True,
                                tile_position=(64, 0)),
                            deps=[mmA], tag=f"sB{hp}_{qb}_{tc}")
                        a0 = p.op(
                            "vector", lambda v, sp=sp, e=g % 2:
                            v.tensor_copy(
                                i8stv[e][:, sp * 1024:(sp + 1) * 1024],
                                pbk(2 * sp, 1024)),
                            deps=[mmB, prev_exp_par[g % 2]]
                            + ([last_cv_tok] if hp == 0 and qb == 0
                               and g <= 1 else []),
                            tag=f"a0_{hp}_{qb}_{tc}")
                        a0s[tc] = a0
                        sc_last = a0
                    ex = p.op(
                        "scalar", lambda s, e=g % 2: s.activation(
                            estv[e], i8stv[e], ACTF.Exp, bias=0.0,
                            scale=v_sattn[0:P, 0:1]),
                        deps=[a0s[tc0], a0s[tc1], b_st,
                              av_par[g % 2],
                              vev_toks[-1] if hp == 0 and qb == 0 and g <= 1
                              else None,
                              last_cv_tok if hp == 0 and qb == 0 and g <= 1
                              else None],
                        tag=f"ex{hp}_{qb}_{g}")
                    prev_exp_par[g % 2] = ex
                    ex_last = ex
                    avdepv = [rvw] if tc1 >= 16 else []
                    pb = 4 + 2 * (qb % 2)
                    avA = p.op(
                        "tensor", lambda t, hp=hp, tc0=tc0, g=g, pb=pb,
                        e=g % 2: t.matmul(
                            pbk(pb, QB, 65), vslice(tc0, 2 * hp),
                            estv[e][:, 0:QB], start=(g == 0), stop=False),
                        deps=[ex, avA,
                              last_oiq_par[qb % 2][0] if g == 0 else None]
                        + avdepv, tag=f"avA{hp}_{qb}_{tc0}")
                    avB = p.op(
                        "tensor", lambda t, hp=hp, tc0=tc0, g=g, pb=pb,
                        e=g % 2: t.matmul(
                            pbk(pb + 1, QB, 65), vslice(tc0, 2 * hp + 1),
                            estv[e][:, QB:2 * QB], start=(g == 0),
                            stop=False),
                        deps=[avA,
                              last_oiq_par[qb % 2][1] if g == 0 else None],
                        tag=f"avB{hp}_{qb}_{tc0}")
                    avA = p.op(
                        "tensor", lambda t, hp=hp, tc1=tc1, g=g, pb=pb,
                        e=g % 2: t.matmul(
                            pbk(pb, QB, 65), vslice(tc1, 2 * hp),
                            estv[e][:, 1024:1024 + QB], start=False,
                            stop=(g == NTC // 2 - 1)),
                        deps=[avB], tag=f"avA{hp}_{qb}_{tc1}")
                    avB = p.op(
                        "tensor", lambda t, hp=hp, tc1=tc1, g=g, pb=pb,
                        e=g % 2: t.matmul(
                            pbk(pb + 1, QB, 65), vslice(tc1, 2 * hp + 1),
                            estv[e][:, 1024 + QB:2048], start=False,
                            stop=(g == NTC // 2 - 1)),
                        deps=[avA], tag=f"avB{hp}_{qb}_{tc1}")
                    av_par[g % 2] = avB
                av_last = avB
                zslA = ar.view(O_XQF + 6144, F32, 1, QB, row0=64)
                zscA = ar.view(O_XQF + 4096, F32, 1, QB, row0=64)
                zslB = ar.view(O_XQF + 2048, F32, 1, QB, row0=64)
                zscB = ar.view(O_XQF, F32, 1, QB, row0=64)
                rcA = p.op("vector", lambda v, zslA=zslA, pb=pb:
                           v.reciprocal(zslA, pbk(pb, QB, 1, 64)),
                           deps=[avA, prev_zread[0]], tag=f"zrcA{hp}_{qb}")
                rcB = p.op("vector", lambda v, zslB=zslB, pb=pb:
                           v.reciprocal(zslB, pbk(pb + 1, QB, 1, 64)),
                           deps=[avB, rcA, prev_zread[1]],
                           tag=f"zrcB{hp}_{qb}")
                rsA = p.op("vector", lambda v, zslA=zslA: v.tensor_scalar(
                    zslA, zslA, v_r50[64:65, 0:1], None, ALU.mult),
                    deps=[rcA, b_r50], tag=f"zsA{hp}_{qb}")
                rsB = p.op("vector", lambda v, zslB=zslB: v.tensor_scalar(
                    zslB, zslB, v_r50[64:65, 0:1], None, ALU.mult),
                    deps=[rcB, b_r50], tag=f"zsB{hp}_{qb}")
                bcA = p.dma("sync", zbA,
                            ar.bview(O_XQF + 6144, F32, 64, QB, row0=64),
                            deps=[rsA, last_oiq_any[0]], group="zba",
                            tag=f"zbA{hp}_{qb}")
                bcB = p.dma("gpsimd", zbB,
                            ar.bview(O_XQF + 2048, F32, 64, QB, row0=64),
                            deps=[rsB, last_oiq_any[1]], group="zbb",
                            tag=f"zbB{hp}_{qb}")
                oA = p.op("vector", lambda v, hp=hp, qsl=qsl, pb=pb:
                          v.tensor_tensor(
                              outiq(hp)[0:64, qsl], pbk(pb, QB, 64), zbA,
                              ALU.mult),
                          deps=[bcA, avB], tag=f"oA{hp}_{qb}")
                oB = p.op("vector", lambda v, hp=hp, qsl=qsl, pb=pb:
                          v.tensor_tensor(
                              outiq(hp)[64:P, qsl], pbk(pb + 1, QB, 64), zbB,
                              ALU.mult),
                          deps=[bcB, oA], tag=f"oB{hp}_{qb}")
                oiq_toks.append((oA, oB))
                last_oiq_par[qb % 2] = [oA, oB]
                last_oiq_any = [oA, oB]
                prev_zread = [bcA, bcB]
            last_khp_use[par] = sc_last

        prj_stores = []
        last_wave = None
        last_cvt = None
        prev_pst = None
        for qb in range(NQB):
            qsl = slice(qb * QB, (qb + 1) * QB)
            lds = []
            for j in range(DC):
                lds.append(p.dma(
                    "sync", xio(j), xs1[j][:, qsl],
                    deps=[ksend_done, oiq_toks[-1][1]]
                    + ([prev_pst] if prev_pst else []),
                    group="px", tag=f"pxl{qb}_{j}"))
            ldall = lds[-1]
            badd = []
            for j in range(DC):
                badd.append(p.op(
                    "vector", lambda v, j=j: v.tensor_scalar(
                        xio(j), xio(j), v_bp[:, j:j + 1], None, ALU.add),
                    deps=[ldall, iv_last], tag=f"pb{qb}_{j}"))
            cvt = []
            for jf in range(DC):
                cvt.append(p.op(
                    "vector", lambda v, jf=jf, qsl=qsl: v.tensor_copy(
                        oq16(jf), outiq(jf)[:, qsl]),
                    deps=[oiq_toks[jf * NQB + qb][1], last_wave,
                          vsc_toks[1], last_khp_use[0], last_khp_use[1]],
                    tag=f"pcv{qb}_{jf}"))
            for w in range(2):
                mm = None
                mm_by_fc = {}
                for jf in range(DC):
                    for fc in range(4 * w, 4 * w + 4):
                        mm = p.op(
                            "tensor", lambda t, jf=jf, fc=fc, w=w: t.matmul(
                                pbk(fc - 4 * w, QB),
                                wpT(jf)[:, fc * P:(fc + 1) * P], oq16(jf),
                                start=(jf == 0), stop=(jf == DC - 1)),
                            deps=[cvt[jf], mm, wq_ready, last_wave]
                            + ([av_last, sc_last] if (qb, w, jf, fc) ==
                               (0, 0, 0, 0) else []),
                            tag=f"pmm{qb}_{w}_{jf}_{fc}")
                        mm_by_fc[fc] = mm
                evs = []
                for fc in range(4 * w, 4 * w + 4):
                    evs.append(p.op(
                        "vector", lambda v, fc=fc, w=w: v.scalar_tensor_tensor(
                            xio(fc), pbk(fc - 4 * w, QB),
                            v_scp[:, fc:fc + 1], xio(fc),
                            ALU.mult, ALU.add),
                        deps=[mm, badd[fc], s_scp], tag=f"pev{qb}_{w}_{fc}"))
                last_wave = evs[-1]
                for fc in range(4 * w, 4 * w + 4):
                    prj_stores.append(p.dma(
                        "gpsimd", xs2[fc][:, qsl], xio(fc),
                        deps=[evs[fc - 4 * w]], group="ps",
                        tag=f"pst{qb}_{fc}"))
            prev_pst = prj_stores[-1]

        w1r_last = None
        for j in range(DC):
            w1r_last = p.dma("sync", w1t(j), w1scr[j],
                             deps=[av_last, w1s_last, vsend_done],
                             group="w1r", tag=f"w1r{j}")
        f2bar = [prj_stores[-1], w1r_last, ex_last, av_last,
                 last_wave, oiq_toks[-1][1], sc_last]
        f2 = emit_ffn(xs2v, dyv, f2bar, "F2", prj_stores[-1], w2_ready)

        p.barrier_tokens(f2["stores"])
        p.op("sync", lambda s: None, deps=f2["stores"], tag="fin")

        sems = {nm: ee(nc.semaphore("s_" + nm)) for nm in p.sem_names()}
        p.build(block, sems)

    nc.compile()
    return nc


_NC_CACHE = None


def kernel(**inputs):
    global _NC_CACHE
    x = np.asarray(inputs["L0_attn_x"], np.float32)
    wqkv = np.asarray(inputs["L0_attn_W_qkv"], np.float32)
    bqkv = np.asarray(inputs["L0_attn_b_qkv"], np.float32)
    wp = np.asarray(inputs["L0_attn_W_proj"], np.float32)
    bp = np.asarray(inputs["L0_attn_b_proj"], np.float32)
    sw = np.asarray(inputs["L0_attn_s_w_qkv"], np.float32)
    zw = np.asarray(inputs["L0_attn_zp_w_qkv"], np.float32)
    swp = np.asarray(inputs["L0_attn_s_w_proj"], np.float32)
    zwp = np.asarray(inputs["L0_attn_zp_w_proj"], np.float32)
    w1 = np.asarray(inputs["L0_ffn_w1"], np.float32)
    b1 = np.asarray(inputs["L0_ffn_b1"], np.float32)
    w2 = np.asarray(inputs["L0_ffn_w2"], np.float32)
    b2 = np.asarray(inputs["L0_ffn_b2"], np.float32)
    s_aq = float(np.asarray(inputs["L0_attn_s_act_qkv"]))
    zp_aq = float(np.asarray(inputs["L0_attn_zp_act_qkv"]))
    s_at = float(np.asarray(inputs["L0_attn_s_attn"]))
    zp_at = float(np.asarray(inputs["L0_attn_zp_attn"]))
    s_ap = float(np.asarray(inputs["L0_attn_s_act_proj"]))
    zp_ap = float(np.asarray(inputs["L0_attn_zp_act_proj"]))
    assert zp_aq == 128.0 and zp_at == 128.0 and zp_ap == 128.0, (
        "kernel assumes zero points of 128 (as produced by setup_inputs)")

    if _NC_CACHE is None:
        _NC_CACHE = build_nc()
    nc = _NC_CACHE

    common = {
        "w1t": np.ascontiguousarray(w1.T),
        "w2t": np.ascontiguousarray(w2.T),
        "wqkvT": np.ascontiguousarray(wqkv.T),
        "wpT": np.ascontiguousarray(wp.T),
        "bqkv": np.ascontiguousarray(bqkv.reshape(24, P).T),
        "bp": np.ascontiguousarray(bp.reshape(8, P).T),
        "b1": np.ascontiguousarray(b1.reshape(32, P).T),
        "b2": np.ascontiguousarray(b2.reshape(8, P).T),
        "sw": np.ascontiguousarray(sw.reshape(24, P).T),
        "swp": np.ascontiguousarray(swp.reshape(8, P).T),
        "swrow": sw.reshape(1, 3 * D),
        "zwrow": zw.reshape(1, 3 * D),
        "swprow": swp.reshape(1, D),
        "zwprow": zwp.reshape(1, D),
        "bqrow": bqkv.reshape(1, 3 * D),
        "scal": np.array([[s_aq, s_at, s_ap]], np.float32),
    }
    in_maps = []
    for c in range(8):
        b, h = c // 2, c % 2
        xc = np.ascontiguousarray(x[b, h * TOK:(h + 1) * TOK, :].T)
        m = dict(common)
        m["x"] = xc
        in_maps.append(m)

    res = run_bass_kernel_spmd(nc, in_maps, core_ids=list(range(8)))
    out = np.empty((4, 4096, D), np.float32)
    for c in range(8):
        b, h = c // 2, c % 2
        out[b, h * TOK:(h + 1) * TOK, :] = np.asarray(res.results[c]["y"]).T
    return out
